# version 03
# baseline (speedup 1.0000x reference)
from contextlib import ExitStack

import numpy as np
import ml_dtypes

import concourse.bass as bass
import concourse.mybir as mybir
from concourse.bass_utils import run_bass_kernel_spmd

F32 = mybir.dt.float32
BF16 = mybir.dt.bfloat16
FP16 = mybir.dt.float16
U8 = mybir.dt.uint8
AF = mybir.ActivationFunctionType
ALU = mybir.AluOpType
AX = mybir.AxisListType

MAGIC = 12582912.0
T = 4096
D = 1024
H = 8
DH = 128
I = 2816
NI = I // 128
MC = D // 128
NTT = T // 128
NTB = T // 512
EPS_LN = 1e-5
EPS_RMS = 1e-6

ENGS = ("sync", "vector", "scalar", "tensor", "gpsimd")
SEMS = ("q", "qa", "qb", "qc", "qd", "q2a", "q2b", "ga", "gb", "gm", "gr", "go",
        "outa", "outb", "cc", "t", "v", "s", "vv", "ss")

_STAGES = {"p1", "p1t", "ar1", "p2", "p3", "p35", "p4", "p5", "p6"}
_DBG = {}


class Sched:
    def __init__(self):
        self.ops = {e: [] for e in ENGS}
        self.cnt = {}

    def val(self, sem):
        return self.cnt.get(sem, 0)

    def op(self, eng, fn, inc=None, by=1, waits=()):
        v = None
        if inc is not None:
            v = self.cnt.get(inc, 0) + by
            self.cnt[inc] = v
        self.ops[eng].append((fn, inc, by, tuple(waits)))
        return v

    def wait(self, eng, *deps):
        self.ops[eng].append((None, None, 0, tuple(deps)))

    def allw(self, *skip):
        return tuple((s, self.val(s)) for s in SEMS if s not in skip and self.val(s) > 0)

    def barrier(self):
        w = self.allw()
        for e in ENGS:
            self.wait(e, *w)

    def emit(self, block, semh):
        def mk_body(eng_name):
            ops = self.ops[eng_name]

            def body(E):
                for fn, inc, by, waits in ops:
                    for s, v in waits:
                        if v > 0:
                            E.wait_ge(semh[s], v)
                    if fn is not None:
                        ins = fn(E)
                        if inc is not None:
                            ins.then_inc(semh[inc], by)
            return body

        block.sync(mk_body("sync"))
        block.vector(mk_body("vector"))
        block.scalar(mk_body("scalar"))
        block.tensor(mk_body("tensor"))
        block.gpsimd(mk_body("gpsimd"))


def build_program(dbg=()):
    nc = bass.Bass("TRN2", target_bir_lowering=False, debug=False)
    dbg = set(dbg)

    def din(name, shape, dt):
        return nc.dram_tensor(name, list(shape), dt, kind="ExternalInput").ap()

    x_d = din("x", (T, D), F32)
    wqk_d = din("wqk", (2, H, 128, MC, 128), F32)
    wv_d = din("wv", (2, 128, MC, 512), F32)
    wo_d = din("wo", (128, H, D), F32)
    wgn_d = din("wgn", (I, D), F32)
    wgt_d = din("wgt", (D, I), F32)
    wun_d = din("wun", (I, D), F32)
    wut_d = din("wut", (D, I), F32)
    wdn_d = din("wdn", (D, I), F32)
    wdt_d = din("wdt", (I, D), F32)
    clipx_d = din("clipx", (1, 1), F32)
    clipd_d = din("clipd", (1, 1), F32)
    tabC_d = din("tabC", (128, T), BF16)
    tabS_d = din("tabS", (128, T), BF16)
    masks_d = din("masks", (128, 2048), BF16)
    ident_d = din("ident", (128, 128), BF16)
    swapm_d = din("swapm", (128, 128), BF16)
    onescol_d = din("onescol", (128, 1), BF16)
    onesrow_d = din("onesrow", (1, 128), BF16)
    onesrowf_d = din("onesrowf", (1, 128), F32)
    ones8f_d = din("ones8f", (1, 8), F32)
    consts_d = din("consts", (128, 4), F32)

    out_d = nc.dram_tensor("out", [T, D], F32, kind="ExternalOutput").ap()
    xnt_in_d = din("xnt_in", (128, MC * T), BF16) if _DBG.get("fakexnt") else None

    xnsp = nc.dram_tensor("xnsp", [T, D], F32, kind="Internal").ap()
    hsp = nc.dram_tensor("hsp", [NI, NTT, 128, 128], FP16, kind="Internal").ap()
    otsp = nc.dram_tensor("otsp", [H, NTT, 128, 128], BF16, kind="Internal").ap()
    rowscr = nc.dram_tensor("rowscr", [2, T], BF16, kind="Internal").ap()
    rowscr2 = nc.dram_tensor("rowscr2", [2, T], BF16, kind="Internal").ap()
    wscr = nc.dram_tensor("wscr", [1, 4096], F32, kind="Internal").ap()
    mxscr = nc.dram_tensor("mxscr", [1, 128], F32, kind="Internal").ap()
    cc1i = nc.dram_tensor("cc1i", [1, 8], F32, kind="Internal")
    cc1o = nc.dram_tensor("cc1o", [1, 8], F32, kind="Internal")
    cc2i = nc.dram_tensor("cc2i", [1, 8], F32, kind="Internal")
    cc2o = nc.dram_tensor("cc2o", [1, 8], F32, kind="Internal")

    dbg_outs = {}
    for name, shape, dt in (
        ("dbg_xnt", (128, MC * T), BF16),
        ("dbg_qt", (2, 128, T), BF16),
        ("dbg_v", (128, T), BF16),
        ("dbg_ot", (H, NTT, 128, 128), BF16),
        ("dbg_attn", (128, NTT * D), BF16),
        ("dbg_xqt", (128, MC * T), BF16),
        ("dbg_gq", (128, MC * I), BF16),
        ("dbg_uq", (128, MC * I), BF16),
        ("dbg_dq", (128, NI * D), BF16),
        ("dbg_h", (NI, NTT, 128, 128), FP16),
        ("dbg_sc", (1, 16), F32),
    ):
        if name in dbg:
            dbg_outs[name] = nc.dram_tensor(name, list(shape), dt, kind="ExternalOutput").ap()

    S = Sched()

    with ExitStack() as stk:
        E = stk.enter_context
        A_BIG = E(nc.sbuf_tensor("A_BIG", [128, 65536], U8))
        A_W = E(nc.sbuf_tensor("A_W", [128, 90112], U8))
        A_ACT = E(nc.sbuf_tensor("A_ACT", [128, 22528], U8))
        A_MISC = E(nc.sbuf_tensor("A_MISC", [128, 12064], U8))
        sb_sc = E(nc.sbuf_tensor("sb_sc", [1, 16], F32))
        sb_r128 = E(nc.sbuf_tensor("sb_r128", [1, 128], F32))
        sb_r8 = E(nc.sbuf_tensor("sb_r8", [1, 8], F32))
        sb_c128 = E(nc.sbuf_tensor("sb_c128", [128, 8], F32))
        tabC = E(nc.sbuf_tensor("sb_tabC", [128, T], BF16))
        tabS = E(nc.sbuf_tensor("sb_tabS", [128, T], BF16))
        masks = E(nc.sbuf_tensor("sb_masks", [128, 2048], BF16))
        ident = E(nc.sbuf_tensor("sb_ident", [128, 128], BF16))
        swapm = E(nc.sbuf_tensor("sb_swapm", [128, 128], BF16))
        onescol = E(nc.sbuf_tensor("sb_onescol", [128, 1], BF16))
        onesrow = E(nc.sbuf_tensor("sb_onesrow", [1, 128], BF16))
        onesrowf = E(nc.sbuf_tensor("sb_onesrowf", [1, 128], F32))
        ones8f = E(nc.sbuf_tensor("sb_ones8f", [1, 8], F32))
        consts = E(nc.sbuf_tensor("sb_consts", [128, 4], F32))

        psA = E(nc.psum_tensor("psA", [128, 512], F32))
        psB = E(nc.psum_tensor("psB", [128, 512], F32))
        psO = E(nc.psum_tensor("psO", [128, 512], F32))
        psR = E(nc.psum_tensor("psR", [128, 512], F32))
        psBC = E(nc.psum_tensor("psBC", [128, 512], F32))
        psQ1 = E(nc.psum_tensor("psQ1", [128, 512], F32))
        psQ2 = E(nc.psum_tensor("psQ2", [128, 512], F32))
        psMS = E(nc.psum_tensor("psMS", [128, 512], F32))

        sems = {}
        for sname in SEMS:
            sems[sname] = E(nc.semaphore(sname))
        block = E(nc.Block())

        def bview(arena, off, nelem, dt):
            size = {F32: 4, BF16: 2, FP16: 2}[dt]
            return arena[:, off:off + nelem * size].bitcast(dt)

        def big(off, n, dt=BF16):
            return bview(A_BIG, off, n, dt)

        def wreg(off, n, dt):
            return bview(A_W, off, n, dt)

        def act(off, n, dt):
            return bview(A_ACT, off, n, dt)

        def misc(off, n, dt):
            return bview(A_MISC, off, n, dt)

        RM_OFF, GSC_OFF, USC_OFF, WDSC_OFF = 11264, 11352, 11440, 11528
        ST_OFF, SC_OFF = 11584, 11600
        MSQ_OFF, MSQB_OFF = 11712, 11840
        INV1_OFF, INV2_OFF = 11968, 12032
        HMX_OFF, HMXT_OFF = 11904, 11936
        R128_OFF, R8_OFF = 12224, 12736

        stats = misc(ST_OFF, 4, F32)
        scal = sb_sc

        def scslot(i):
            return scal[0:1, i:i + 1]

        def rowap(part, cols, dt, byteoff=0):
            size = {F32: 4, BF16: 2, FP16: 2}[dt]
            return A_MISC[part:part + 1, byteoff:byteoff + cols * size].bitcast(dt)

        s1pair = misc(INV1_OFF, 2, F32)
        s2pair = misc(INV2_OFF, 2, F32)
        inv1vec = s1pair[:, 1:2]
        inv2vec = s2pair[:, 1:2]
        msq = misc(MSQ_OFF, 32, F32)
        msqb = misc(MSQB_OFF, 32, BF16)
        hmx = misc(HMX_OFF, 1, F32)
        hmxt = misc(HMXT_OFF, 1, F32)
        row128 = sb_r128
        row8 = sb_r8

        def dma(eng, sem, out, in_, waits=()):
            return S.op(eng, lambda E_, o=out, i=in_: E_.dma_start(out=o, in_=i),
                        inc=sem, by=16, waits=waits)

        for (t_, d_) in ((tabC, tabC_d), (tabS, tabS_d), (masks, masks_d), (ident, ident_d),
                         (swapm, swapm_d),
                         (onescol, onescol_d), (onesrow, onesrow_d), (onesrowf, onesrowf_d),
                         (ones8f, ones8f_d), (consts, consts_d)):
            dma("sync", "q", t_[:], d_)
        v_consts = 16 * 10

        xnT = big(0, MC * T)
        if _DBG.get("fakexnt"):
            dma("sync", "q", xnT[:], xnt_in_d)

        xtile = [act(0, D, F32), act(4096, D, F32)]
        xn32 = [act(8192, D, F32), act(12288, D, F32)]
        xnbf = [act(16384, D, BF16), act(18432, D, BF16)]
        mxacc = misc(11600, 1, F32)
        mxtmp = misc(11632, 1, F32)

        v_mxg = 0
        if "p1" in _STAGES:
            last_sp = [0, 0]
            last_stats_v = 0
            macc = 0
            last_cb = [0, 0]
            last_tp = [0, 0]
            last_ln = [0, 0]
            tguard = 0
            for tt in range(NTT):
                b = tt % 2
                qs = "qa" if b == 0 else "qb"
                gs = "ga" if b == 0 else "gb"
                lv = dma("sync", qs, xtile[b][:], x_d[128 * tt:128 * (tt + 1), :],
                         waits=(("s", last_ln[b]),))
                r1 = S.op("vector", lambda E_, xb=xtile[b], q_=qs: E_.reduce_sum(
                    out=stats[:, 0:1], in_=xb[:], axis=AX.X), inc="vv",
                    waits=((qs, lv), ("v", last_stats_v)))
                sv = S.op("scalar", lambda E_, xb=xtile[b], du=xn32[b]: E_.activation(
                    out=du[:], in_=xb[:], func=AF.Square, accum_out=stats[:, 1:2]),
                    inc="s", waits=((qs, lv), (gs, last_sp[b]), ("v", last_cb[b]),
                                    ("v", last_stats_v)))
                r2 = S.op("vector", lambda E_: E_.tensor_scalar_mul(
                    out=stats[:, 0:1], in0=stats[:, 0:1], scalar1=1.0 / D), inc="vv",
                    waits=(("s", sv), ("vv", r1)))
                r3 = S.op("vector", lambda E_: E_.tensor_tensor(
                    out=stats[:, 2:3], in0=stats[:, 0:1], in1=stats[:, 0:1], op=ALU.mult),
                    inc="vv", waits=(("vv", r2),))
                vv = S.op("vector", lambda E_: E_.scalar_tensor_tensor(
                    out=stats[:, 1:2], in0=stats[:, 1:2], scalar=1.0 / D, in1=stats[:, 2:3],
                    op0=ALU.mult, op1=ALU.subtract), inc="v", waits=(("vv", r3),))
                l1 = S.op("scalar", lambda E_: E_.activation(
                    out=stats[:, 2:3], in_=stats[:, 1:2], func=AF.Ln, bias=consts[:, 1:2]),
                    inc="ss", waits=(("v", vv), ("q", v_consts)))
                sv = S.op("scalar", lambda E_: E_.activation(
                    out=stats[:, 2:3], in_=stats[:, 2:3], func=AF.Exp, scale=-0.5), inc="s",
                    waits=(("ss", l1),))
                vv = S.op("vector", lambda E_: E_.scalar_tensor_tensor(
                    out=stats[:, 3:4], in0=stats[:, 0:1], scalar=-1.0, in1=stats[:, 2:3],
                    op0=ALU.mult, op1=ALU.mult), inc="v", waits=(("s", sv),))
                last_stats_v = vv
                lnv = S.op("scalar", lambda E_, xb=xtile[b], ob=xn32[b]: E_.activation(
                    out=ob[:], in_=xb[:], func=AF.Identity, scale=stats[:, 2:3],
                    bias=stats[:, 3:4]), inc="s", waits=(("v", vv),))
                last_ln[b] = lnv
                last_sp[b] = S.op("gpsimd", lambda E_, ob=xn32[b], t=tt: E_.dma_start(
                    out=xnsp[128 * t:128 * (t + 1), :], in_=ob[:]), inc=gs, by=16,
                    waits=(("s", lnv),))
                ra = S.op("vector", lambda E_, ob=xn32[b]: E_.reduce_max(
                    out=mxtmp[:], in_=ob[:], axis=AX.X, apply_absolute_value=True),
                    inc="vv", waits=(("s", lnv), ("vv", macc)))
                if tt == 0:
                    macc = S.op("vector", lambda E_: E_.tensor_copy(
                        out=mxacc[:], in_=mxtmp[:]), inc="vv", waits=(("vv", ra),))
                else:
                    macc = S.op("vector", lambda E_: E_.tensor_tensor(
                        out=mxacc[:], in0=mxacc[:], in1=mxtmp[:], op=ALU.max),
                        inc="vv", waits=(("vv", ra),))
                if "p1t" in _STAGES:
                    cbv = S.op("vector", lambda E_, ob=xn32[b], bb=xnbf[b]: E_.tensor_copy(
                        out=bb[:], in_=ob[:]), inc="v", waits=(("t", last_tp[b]), ("s", lnv)))
                    last_cb[b] = cbv
                    for c in range(MC):
                        pq = psQ1 if c % 2 == 0 else psQ2
                        tv = S.op("tensor", lambda E_, bb=xnbf[b], cc=c, p_=pq: E_.matmul(
                            p_[:, 0:128], bb[:, 128 * cc:128 * (cc + 1)], ident[:],
                            start=True, stop=True), inc="t",
                            waits=(("v", cbv), ("v", tguard), ("q", v_consts)))
                        tguard = S.op("vector", lambda E_, cc=c, t=tt, p_=pq: E_.tensor_copy(
                            out=xnT[:, T * cc + 128 * t:T * cc + 128 * (t + 1)],
                            in_=p_[:, 0:128]), inc="v", waits=(("t", tv),))
                    last_tp[b] = S.val("t")

            v_mx_last = S.val("v")
            vv_mx_last = macc
        if "ar1" in _STAGES:
            gv = S.op("gpsimd", lambda E_: E_.dma_start(
                out=mxscr[0, 0:128], in_=mxacc[:]), inc="gm", by=16,
                waits=(("v", v_mx_last), ("vv", vv_mx_last)))
            gv = S.op("gpsimd", lambda E_: E_.dma_start(
                out=row128[0:1, :], in_=mxscr[0:1, 0:128]), inc="gm", by=16,
                waits=(("gm", gv),))
            vmx = S.op("vector", lambda E_: E_.reduce_max(
                out=scslot(0), in_=row128[0:1, :], axis=AX.X), inc="v",
                waits=(("gm", gv),))
            gv = S.op("gpsimd", lambda E_: E_.dma_start(
                out=cc1i.ap(), in_=ones8f[:]), inc="gm", by=16,
                waits=(("q", v_consts), ("gm", gv)))
            gv = S.op("gpsimd", lambda E_: E_.dma_start(
                out=cc1i.ap()[0, 0:1], in_=scslot(0)), inc="gm", by=16,
                waits=(("v", vmx), ("gm", gv)))
            arv = S.op("gpsimd", lambda E_: E_.collective_compute(
                "AllReduce", ALU.max, replica_groups=[list(range(8))],
                ins=[cc1i.ap().opt()], outs=[cc1o.ap().opt()]), inc="cc",
                waits=(("gm", gv),))
            gv = S.op("gpsimd", lambda E_: E_.dma_start(
                out=row8[0:1, :], in_=cc1o.ap()), inc="gm", by=16, waits=(("cc", arv),))
            v_mxg = S.op("vector", lambda E_: E_.tensor_copy(
                out=scslot(1), in_=row8[0:1, 0:1]), inc="v", waits=(("gm", gv),))

        if "p1" in _STAGES and "dbg_xnt" in dbg_outs:
            S.op("gpsimd", lambda E_: E_.dma_start(
                out=dbg_outs["dbg_xnt"], in_=xnT[:]), inc="gm", by=16,
                waits=(("v", S.val("v")), ("t", S.val("t"))))

        VSLOT = [wreg(8192 * i, T, BF16) for i in range(5)]
        wstage = wreg(40960, 4096, F32)
        wqh = [wreg(57344 + 2048 * i, D, BF16) for i in range(2)]
        wkh = [wreg(61440 + 2048 * i, D, BF16) for i in range(2)]
        m1b = [wreg(65536 + 2048 * i, 512, F32) for i in range(2)]
        wvb = wreg(69632, 4096, BF16)

        qTb = act(0, T, BF16)
        kTb = act(8192, T, BF16)
        PT = [act(16384 + 1024 * i, 512, BF16) for i in range(3)]
        binv = act(19456, 512, BF16)
        obuf = act(20480, 512, BF16)
        sqb = wreg(88064, 512, BF16)
        qsw = wreg(77824, D, BF16)

        p2_ot_g = 0
        if "p2" in _STAGES:
            vslot_evac = [0] * 5
            stage_free_v = 0
            psq_guard_v = 0
            prev_obuf_v = 0
            prev_obuf_g = 0
            last_s2v = 0
            sfree = [0, 0]
            PT_free = [0, 0, 0]
            prev_vv2 = 0
            prev_rope = 0
            psw_guard = [0, 0]
            last_g1 = 0
            w_guard = [0, 0]
            v_p1_all = S.allw()

            def v_batch(b4):
                nonlocal stage_free_v, psq_guard_v
                lv = dma("sync", "qa", wstage[:].rearrange("p (c j) -> p c j", j=512), wv_d[b4],
                         waits=(("v", stage_free_v),) + (v_p1_all if b4 == 0 else ()))
                cv = S.op("vector", lambda E_: E_.tensor_copy(
                    out=wvb[:], in_=wstage[:]), inc="v", waits=(("qa", lv),))
                stage_free_v = cv
                last_evac = cv
                for tt in range(NTT):
                    wts = [("v", last_evac)]
                    if tt == 0 and b4 == 0:
                        wts += list(v_p1_all)
                    if tt == 0 and b4 == 1:
                        wts += [("v", S.val("v")), ("s", S.val("s")),
                                ("gm", S.val("gm"))]
                    for c in range(MC):
                        tvv = S.op("tensor", lambda E_, cc=c, t=tt: E_.matmul(
                            psQ1[:], xnT[:, T * cc + 128 * t:T * cc + 128 * (t + 1)],
                            wvb[:, 512 * cc:512 * (cc + 1)],
                            start=(cc == 0), stop=(cc == MC - 1)),
                            inc="t" if c == MC - 1 else None, waits=wts if c == 0 else ())
                    for hh in range(4):
                        sl = (4 * b4 + hh) % 5
                        last_evac = S.op("vector", lambda E_, s_=sl, hh_=hh, t=tt: E_.tensor_copy(
                            out=VSLOT[s_][:, 128 * t:128 * (t + 1)],
                            in_=psQ1[:, 128 * hh_:128 * (hh_ + 1)]), inc="v",
                            waits=(("t", tvv),))
                        vslot_evac[sl] = last_evac
                psq_guard_v = last_evac

            v_batch(0)

            for h in range(0 if _DBG.get('p2cut') == 'vbatch'
                           else int(_DBG.get('nheads', H))):
                par = h % 2
                lq = dma("sync", "qa", wstage[:, 0:1024],
                         wqk_d[0, h].rearrange("p c d -> p (c d)"),
                         waits=(("v", stage_free_v),))
                lk = dma("sync", "qb", wstage[:, 1024:2048],
                         wqk_d[1, h].rearrange("p c d -> p (c d)"))
                cq = S.op("vector", lambda E_, p_=par: E_.tensor_copy(
                    out=wqh[p_][:], in_=wstage[:, 0:1024]), inc="v",
                    waits=(("qa", lq), ("t", w_guard[par])))
                ck = S.op("vector", lambda E_, p_=par: E_.tensor_copy(
                    out=wkh[p_][:], in_=wstage[:, 1024:2048]), inc="v",
                    waits=(("qb", lk), ("t", w_guard[par])))
                stage_free_v = ck

                if h == 4:
                    v_batch(1)
                skip_flash = bool(_DBG.get('noflash'))

                qT_ready = [0] * NTB
                kT_ready = [0] * NTB
                for qk in range(int(_DBG.get("nqk", 2))):
                    wbuf = wqh[par] if qk == 0 else wkh[par]
                    wready = cq if qk == 0 else ck
                    dst = qTb if qk == 0 else kTb
                    msrow = rowap(0, T, BF16)
                    rope_v = [0] * NTB
                    last_ms_evac = 0
                    cut = _DBG.get("p2cut")
                    for tb in range(int(_DBG.get("ntb", NTB))):
                        pq = psQ1 if tb % 2 == 0 else psQ2
                        wts = [("v", wready)]
                        if tb >= 2:
                            wts.append(("v", rope_v[tb - 2]))
                        else:
                            wts += [("v", S.val("v")), ("s", S.val("s")),
                                    ("gm", S.val("gm"))]
                        for c in range(MC):
                            tvv = S.op("tensor", lambda E_, cc=c, w_=wbuf, p_=pq, t_=tb: E_.matmul(
                                p_[:], w_[:, 128 * cc:128 * (cc + 1)],
                                xnT[:, T * cc + 512 * t_:T * cc + 512 * (t_ + 1)],
                                start=(cc == 0), stop=(cc == MC - 1)),
                                inc="t" if c == MC - 1 else None,
                                waits=wts if c == 0 else ())
                        mmv = tvv
                        if cut == "projonly":
                            rope_v[tb] = S.op("vector", lambda E_: E_.tensor_copy(
                                out=mxtmp[:], in_=mxacc[:]), inc="v",
                                waits=(("t", mmv),))
                            continue
                        sqv = S.op("scalar", lambda E_, p_=pq: E_.activation(
                            out=sqb[:], in_=p_[:], func=AF.Square), inc="s",
                            waits=(("t", mmv),))
                        if cut == "sq2":
                            rope_v[tb] = S.op("vector", lambda E_: E_.tensor_copy(
                                out=mxtmp[:], in_=mxacc[:]), inc="v",
                                waits=(("s", sqv),))
                            continue
                        qrb = qsw[:, 0:512] if tb % 2 == 0 else qsw[:, 512:1024]
                        swv = S.op("scalar", lambda E_, p_=pq, qr=qrb: E_.activation(
                            out=qr, in_=p_[:], func=AF.Copy), inc="s",
                            waits=(("v", prev_vv2),))
                        if cut in ("norope", "sqonly"):
                            rope_v[tb] = S.op("vector", lambda E_: E_.tensor_copy(
                                out=mxtmp[:], in_=mxacc[:]), inc="v",
                                waits=(("s", swv),))
                            continue
                        if cut != "noms":
                            msv = S.op("tensor", lambda E_: E_.matmul(
                                psMS[0:1, :], onescol[:], sqb[:], start=True, stop=True),
                                inc="t", waits=(("s", sqv), ("v", last_ms_evac)))
                            last_ms_evac = S.op("vector", lambda E_, t_=tb, mr=msrow: E_.tensor_copy(
                                out=mr[0:1, 512 * t_:512 * (t_ + 1)], in_=psMS[0:1, :]),
                                inc="v", waits=(("t", msv),) + ((("gr", last_g1),) if tb == 0 else ()))
                        psw = psA if tb % 2 == 0 else psB
                        swt = S.op("tensor", lambda E_, qr=qrb, p_=psw: E_.matmul(
                            p_[:], swapm[:], qr, start=True, stop=True), inc="t",
                            waits=(("s", swv), ("v", psw_guard[tb % 2])))
                        m1v = wreg(65536 + 2048 * (tb % 2), 512, BF16)
                        vv1 = S.op("vector", lambda E_, t_=tb, m_=m1v, qr=qrb: E_.tensor_tensor(
                            out=m_[:], in0=qr, in1=tabC[:, 512 * t_:512 * (t_ + 1)],
                            op=ALU.mult), inc="vv", waits=(("s", swv),))
                        prev_vv2 = psw_guard[tb % 2] = S.op(
                            "vector", lambda E_, t_=tb, p_=psw, d_=dst: E_.tensor_tensor(
                                out=d_[:, 512 * t_:512 * (t_ + 1)], in0=p_[:],
                                in1=tabS[:, 512 * t_:512 * (t_ + 1)], op=ALU.mult),
                            inc="v", waits=(("t", swt), ("v", prev_rope)))
                        rope_v[tb] = prev_rope = S.op(
                            "vector", lambda E_, d_=dst, t_=tb, m_=m1v: E_.tensor_tensor(
                                out=d_[:, 512 * t_:512 * (t_ + 1)],
                                in0=d_[:, 512 * t_:512 * (t_ + 1)], in1=m_[:],
                                op=ALU.add), inc="v", waits=(("vv", vv1), ("v", prev_vv2)))
                    if cut in ("projonly", "noroundtrip"):
                        w_guard[par] = mmv
                        continue
                    g1 = last_g1 = S.op("gpsimd", lambda E_, mr=msrow, k_=qk: E_.dma_start(
                        out=rowscr[k_, :], in_=mr[0:1, :]), inc="gr", by=16,
                        waits=(("v", last_ms_evac), ("t", S.val("t"))))
                    g2 = S.op("gpsimd", lambda E_, k_=qk: E_.dma_start(
                        out=msqb[:], in_=rowscr[k_].rearrange("(p c) -> p c", c=32)),
                        inc="gr", by=16, waits=(("gr", g1), ("s", last_s2v)))
                    sl1 = S.op("scalar", lambda E_: E_.activation(
                        out=msq[:], in_=msqb[:], func=AF.Ln, scale=1.0 / DH,
                        bias=consts[:, 2:3]), inc="ss", waits=(("gr", g2),))
                    last_s2v = S.op("scalar", lambda E_: E_.activation(
                        out=msqb[:], in_=msq[:], func=AF.Exp, scale=-0.5), inc="s",
                        waits=(("ss", sl1),))
                    g3 = S.op("gpsimd", lambda E_, k_=qk: E_.dma_start(
                        out=rowscr2[k_, :], in_=msqb[:]), inc="gr", by=16,
                        waits=(("s", last_s2v),))
                    rsrow = A_W[0:1, 79872:88064].bitcast(BF16)
                    g4 = S.op("gpsimd", lambda E_, k_=qk, rr=rsrow: E_.dma_start(
                        out=rr[0:1, :], in_=rowscr2[k_:k_ + 1, :]), inc="gr", by=16,
                        waits=(("gr", g3), ("t", S.val("t"))))
                    for tb in range(NTB):
                        bcv = S.op("tensor", lambda E_, t_=tb, rr=rsrow: E_.matmul(
                            psBC[:], onesrow[:], rr[0:1, 512 * t_:512 * (t_ + 1)],
                            start=True, stop=True), inc="t",
                            waits=(("gr", g4), ("v", rope_v[tb])))
                        rope_v[tb] = S.op("vector", lambda E_, d_=dst, t_=tb: E_.tensor_tensor(
                            out=d_[:, 512 * t_:512 * (t_ + 1)],
                            in0=d_[:, 512 * t_:512 * (t_ + 1)], in1=psBC[:], op=ALU.mult),
                            inc="v", waits=(("t", bcv),))
                        S.wait("tensor", ("v", rope_v[tb]))
                    w_guard[par] = mmv
                    if qk == 0:
                        qT_ready = rope_v[:]
                    else:
                        kT_ready = rope_v[:]

                if "dbg_qt" in dbg_outs and h == _DBG.get("head", 0):
                    S.op("gpsimd", lambda E_: E_.dma_start(
                        out=dbg_outs["dbg_qt"][0], in_=qTb[:]), inc="gm", by=16,
                        waits=(("v", qT_ready[NTB - 1]),))
                    S.op("gpsimd", lambda E_: E_.dma_start(
                        out=dbg_outs["dbg_qt"][1], in_=kTb[:]), inc="gm", by=16,
                        waits=(("v", kT_ready[NTB - 1]),))
                if "dbg_v" in dbg_outs and h == _DBG.get("head", 0):
                    S.op("gpsimd", lambda E_, s_=h % 5: E_.dma_start(
                        out=dbg_outs["dbg_v"], in_=VSLOT[s_][:]), inc="gm", by=16,
                        waits=(("v", vslot_evac[h % 5]),))

                if _DBG.get('noflash'):
                    prev_head_dummy = 0
                vsl = VSLOT[h % 5]
                for qb in range(0 if not _DBG.get('noflash') else NTB, NTB):
                    nkt = 4 * qb + 4
                    for kt in range(nkt):
                        ps_s = psA if kt % 2 == 0 else psB
                        smm = S.op("tensor", lambda E_, k_=kt, q_=qb, p_=ps_s: E_.matmul(
                            p_[:], kTb[:, 128 * k_:128 * (k_ + 1)],
                            qTb[:, 512 * q_:512 * (q_ + 1)], start=True, stop=True),
                            inc="t", waits=(("v", kT_ready[kt // 4]),
                                            ("v", qT_ready[qb]), ("s", sfree[kt % 2])))
                        pbuf = PT[kt % 3]
                        ev = S.op("scalar", lambda E_, p_=ps_s, pb=pbuf: E_.activation(
                            out=pb[:], in_=p_[:], func=AF.Exp, scale=1.0 / DH), inc="s",
                            waits=(("t", smm), ("t", PT_free[kt % 3])))
                        sfree[kt % 2] = ev
                        pready = ("s", ev)
                        j = kt - 4 * qb
                        if j >= 0:
                            mv = S.op("vector", lambda E_, pb=pbuf, j_=j: E_.tensor_tensor(
                                out=pb[:], in0=pb[:], in1=masks[:, 512 * j_:512 * (j_ + 1)],
                                op=ALU.mult), inc="v", waits=(("s", ev),))
                            pready = ("v", mv)
                        S.op("tensor", lambda E_, k_=kt, pb=pbuf, v_=vsl, st_=(kt == 0),
                             sp_=(kt == nkt - 1): E_.matmul(
                            psO[:], v_[:, 128 * k_:128 * (k_ + 1)], pb[:],
                            start=st_, stop=sp_),
                            waits=(pready,) + ((("v", prev_obuf_v),) if kt == 0 else ()))
                        PT_free[kt % 3] = S.op("tensor", lambda E_, pb=pbuf,
                             st_=(kt == 0), sp_=(kt == nkt - 1): E_.matmul(
                            psR[0:1, :], onescol[:], pb[:],
                            start=st_, stop=sp_), inc="t")
                    def _rcopy(E_):
                        with nc.allow_low_precision(reason="r row feeds bf16 PE bcast"):
                            return E_.tensor_copy(
                                out=rowap(0, 512, BF16, byteoff=8192)[0:1, :],
                                in_=psR[0:1, :])
                    rv = S.op("vector", _rcopy, inc="v",
                              waits=(("t", PT_free[(nkt - 1) % 3]),))
                    bcv = S.op("tensor", lambda E_: E_.matmul(
                        psBC[:], onesrow[:], rowap(0, 512, BF16, byteoff=8192)[0:1, :],
                        start=True, stop=True), inc="t", waits=(("v", rv),))

                    def _rinv(E_):
                        with nc.allow_low_precision(reason="1/r bcast in bf16"):
                            return E_.reciprocal(out=binv[:], in_=psBC[:])
                    sbv = S.op("vector", _rinv, inc="v", waits=(("t", bcv),))
                    S.wait("tensor", ("v", sbv))
                    prev_obuf_v = S.op("vector", lambda E_: E_.tensor_tensor(
                        out=obuf[:], in0=psO[:], in1=binv[:], op=ALU.mult), inc="v",
                        waits=(("v", sbv), ("go", prev_obuf_g)))
                    prev_obuf_g = S.op("gpsimd", lambda E_, h_=h, q_=qb: E_.dma_start(
                        out=otsp[h_, 4 * q_:4 * q_ + 4].rearrange("t p f -> p t f"),
                        in_=obuf[:].rearrange("p (t f) -> p t f", f=128)), inc="go", by=16,
                        waits=(("v", prev_obuf_v),))
            p2_ot_g = prev_obuf_g
            if "dbg_ot" in dbg_outs:
                S.op("gpsimd", lambda E_: E_.dma_start(
                    out=dbg_outs["dbg_ot"], in_=otsp), inc="gm", by=16,
                    waits=(("go", S.val("go")),))

        if "p3" in _STAGES:
            lcx = dma("sync", "qc", scslot(2), clipx_d, waits=S.allw("q"))
            tb1 = S.op("tensor", lambda E_: E_.matmul(
                psBC[:, 0:2], onesrowf[:], scal[0:1, 1:3], start=True, stop=True),
                inc="t", waits=(("qc", lcx), ("v", v_mxg)) + S.allw("t", "v"))
            c128 = sb_c128
            c1 = S.op("vector", lambda E_: E_.tensor_copy(
                out=c128[:, 0:2], in_=psBC[:, 0:2]), inc="vv", waits=(("t", tb1),))
            sv = S.op("scalar", lambda E_: E_.activation(
                out=c128[:, 2:3], in_=c128[:, 1:2], func=AF.Exp, scale=-1.0), inc="s",
                waits=(("vv", c1),))
            c1 = S.op("vector", lambda E_: E_.tensor_scalar_add(
                out=c128[:, 2:3], in0=c128[:, 2:3], scalar1=1.0), inc="vv",
                waits=(("s", sv),))
            c1 = S.op("vector", lambda E_: E_.reciprocal(
                out=c128[:, 3:4], in_=c128[:, 2:3]), inc="vv", waits=(("vv", c1),))
            c1 = S.op("vector", lambda E_: E_.tensor_tensor(
                out=c128[:, 3:4], in0=c128[:, 3:4], in1=c128[:, 0:1], op=ALU.mult),
                inc="vv", waits=(("vv", c1),))
            c1 = S.op("vector", lambda E_: E_.tensor_scalar_mul(
                out=s1pair[:, 0:1], in0=c128[:, 3:4], scalar1=1.0 / 127.0), inc="vv",
                waits=(("vv", c1),))
            v_inv1 = S.op("vector", lambda E_: E_.reciprocal(
                out=s1pair[:, 1:2], in_=s1pair[:, 0:1]), inc="v", waits=(("vv", c1),))

            xq_bf = [act(16384, D, BF16), act(18432, D, BF16)]
            v_p2t_all = S.val("t")
            last_q2 = [0, 0]
            last_s1 = [0, 0]
            last_v2v = [0, 0]
            p2_guards = S.allw("q")
            for tt in range(NTT):
                b = tt % 2
                qs = "qa" if b == 0 else "qb"
                q2s = "q2a" if b == 0 else "q2b"
                wts = [("s", last_s1[b]), ("v", last_v2v[b])]
                if tt < 2:
                    wts += list(p2_guards)
                lv = dma("sync", qs, xtile[b][:], xnsp[128 * tt:128 * (tt + 1), :],
                         waits=wts)
                s1v = S.op("scalar", lambda E_, xb=xtile[b], ob=xn32[b]: E_.activation(
                    out=ob[:], in_=xb[:], func=AF.Identity, scale=inv1vec,
                    bias=consts[:, 0:1]), inc="s",
                    waits=((qs, lv), ("v", v_inv1)))
                last_s1[b] = s1v
                rr1 = S.op("vector", lambda E_, ob=xn32[b]: E_.tensor_scalar(
                    out=ob[:], in0=ob[:], scalar1=-MAGIC, scalar2=127.0,
                    op0=ALU.add, op1=ALU.min), inc="vv", waits=(("s", s1v),))
                q2s = "q2a" if b == 0 else "q2b"
                v2v = S.op("vector", lambda E_, ob=xn32[b], qb_=xq_bf[b]: E_.tensor_scalar_max(
                    out=qb_[:], in0=ob[:], scalar1=-128.0), inc="v",
                    waits=((q2s, last_q2[b]), ("vv", rr1)))
                last_v2v[b] = v2v
                for c in range(MC):
                    twts = [("v", v2v)] if c == 0 else []
                    if tt == 0 and c == 0:
                        twts.append(("t", v_p2t_all))
                    last_q2[b] = S.op("scalar", lambda E_, qb_=xq_bf[b], cc=c, t=tt: E_.dma_start_transpose(
                        xnT[:, T * cc + 128 * t:T * cc + 128 * (t + 1)],
                        qb_[:, 128 * cc:128 * (cc + 1)]), inc=q2s, by=16,
                        waits=twts)
            v_xq_a, v_xq_b = S.val("q2a"), S.val("q2b")
            if "dbg_xqt" in dbg_outs:
                S.op("gpsimd", lambda E_: E_.dma_start(
                    out=dbg_outs["dbg_xqt"], in_=xnT[:]), inc="gm", by=16,
                    waits=(("q2a", v_xq_a), ("q2b", v_xq_b)))

        v_xq_ab = (("q2a", S.val("q2a")), ("q2b", S.val("q2b")))
        if "dbg_sc" in dbg_outs:
            lgv = S.op("gpsimd", lambda E_: E_.dma_start(
                out=row8[0:1, :], in_=cc1o.ap()), inc="gm", by=16,
                waits=(("v", S.val("v")), ("s", S.val("s"))))
            lvv = S.op("vector", lambda E_: E_.tensor_copy(
                out=scal[0:1, 12:13], in_=row8[0:1, 0:1]), inc="v",
                waits=(("gm", lgv),))
            lgv = S.op("gpsimd", lambda E_: E_.dma_start(
                out=row8[0:1, :], in_=cc1i.ap()), inc="gm", by=16,
                waits=(("v", lvv),))
            lvv = S.op("vector", lambda E_: E_.tensor_copy(
                out=scal[0:1, 13:14], in_=row8[0:1, 0:1]), inc="v",
                waits=(("gm", lgv),))
            S.op("gpsimd", lambda E_: E_.dma_start(
                out=dbg_outs["dbg_sc"], in_=scal[0:1, :]), inc="gm", by=16,
                waits=(("v", lvv),))

        def wquant(w_nat_d, w_t_d, dst_off, sc_off, v_prev, q2_guard):
            stage_t = act(0, I, F32)
            stage_n = act(11264, D, F32)
            inv_bc = act(11264, I, F32)
            rmcol = misc(RM_OFF, 22, F32)
            lastv = v_prev
            for ic in range(NI):
                wts = [("v", lastv)]
                if ic == 0 and q2_guard:
                    wts += list(q2_guard)
                lv = dma("sync", "qa", stage_n[:], w_nat_d[128 * ic:128 * (ic + 1), :],
                         waits=wts)
                lastv = S.op("vector", lambda E_, i_=ic: E_.reduce_max(
                    out=rmcol[:, i_:i_ + 1], in_=stage_n[:], axis=AX.X,
                    apply_absolute_value=True), inc="v", waits=(("qa", lv),))
            c2 = S.op("vector", lambda E_: E_.tensor_scalar_mul(
                out=misc(sc_off, 22, F32)[:], in0=rmcol[:], scalar1=1.0 / 127.0),
                inc="vv", waits=(("v", lastv),))
            c2 = S.op("vector", lambda E_: E_.reciprocal(out=rmcol[:], in_=rmcol[:]),
                      inc="vv", waits=(("vv", c2),))
            vv = S.op("vector", lambda E_: E_.tensor_scalar_mul(
                out=rmcol[:], in0=rmcol[:], scalar1=127.0), inc="v", waits=(("vv", c2),))
            gl = 0
            for ic in range(NI):
                gl = S.op("gpsimd", lambda E_, i_=ic: E_.dma_start(
                    out=wscr[0, 128 * i_:128 * (i_ + 1)], in_=rmcol[:, i_:i_ + 1]),
                    inc="gm", by=16, waits=(("v", vv),) if ic == 0 else ())
            invrow = rowap(0, I, F32)
            gl2 = S.op("gpsimd", lambda E_: E_.dma_start(
                out=invrow[0:1, :], in_=wscr[0:1, 0:I]), inc="gm", by=16,
                waits=(("gm", gl),))
            lastbc = 0
            for j in range((I + 511) // 512):
                n = min(512, I - 512 * j)
                tbv = S.op("tensor", lambda E_, j_=j, n_=n: E_.matmul(
                    psBC[:, 0:n_], onesrowf[:], invrow[0:1, 512 * j_:512 * j_ + n_],
                    start=True, stop=True), inc="t",
                    waits=(("gm", gl2), ("v", lastbc)) + (S.allw("t", "v") if j == 0 else ()))
                lastbc = S.op("vector", lambda E_, j_=j, n_=n: E_.tensor_copy(
                    out=inv_bc[:, 512 * j_:512 * j_ + n_], in_=psBC[:, 0:n_]), inc="v",
                    waits=(("t", tbv),))
            lastq = 0
            for c in range(MC):
                lv = dma("sync", "qb", stage_t[:], w_t_d[128 * c:128 * (c + 1), :],
                         waits=(("v", lastq if lastq else lastbc),))
                q1 = S.op("vector", lambda E_: E_.scalar_tensor_tensor(
                    out=stage_t[:], in0=stage_t[:], scalar=1.0, in1=inv_bc[:],
                    op0=ALU.mult, op1=ALU.mult), inc="vv",
                    waits=(("qb", lv), ("v", lastbc)))
                lastq = S.op("vector", lambda E_, c_=c: E_.tensor_scalar(
                    out=wreg(dst_off + 5632 * c_, I, BF16)[:], in0=stage_t[:],
                    scalar1=MAGIC, scalar2=-MAGIC, op0=ALU.add, op1=ALU.add), inc="v",
                    waits=(("vv", q1),))
            return lastq

        v_wu = 0
        if "p35" in _STAGES:
            v_wg = wquant(wgn_d, wgt_d, 0, GSC_OFF, 0, v_xq_ab)
            v_wu = wquant(wun_d, wut_d, 45056, USC_OFF, v_wg, ())
            S.op("vector", lambda E_: E_.tensor_single_scalar(
                out=misc(GSC_OFF, 22, F32)[:], in_=misc(GSC_OFF, 22, F32)[:],
                scalar=s1pair[:, 0:1], op=ALU.mult), waits=(("v", v_wu), ("v", v_inv1)))
            v_wu = S.op("vector", lambda E_: E_.tensor_single_scalar(
                out=misc(USC_OFF, 22, F32)[:], in_=misc(USC_OFF, 22, F32)[:],
                scalar=s1pair[:, 0:1], op=ALU.mult), inc="v")
            if "dbg_gq" in dbg_outs:
                S.op("gpsimd", lambda E_: E_.dma_start(
                    out=dbg_outs["dbg_gq"], in_=wreg(0, MC * I, BF16)[:]), inc="gm",
                    by=16, waits=(("v", v_wu),))
                S.op("gpsimd", lambda E_: E_.dma_start(
                    out=dbg_outs["dbg_uq"], in_=wreg(45056, MC * I, BF16)[:]), inc="gm",
                    by=16, waits=(("v", v_wu),))

        v_hmg = 0
        if "p4" in _STAGES:
            hrow = [act(0, T, FP16), act(8192, T, FP16)]
            gsb = [act(16384, 512, BF16), act(18432, 512, BF16)]
            usb = [act(17408, 512, BF16), act(19456, 512, BF16)]
            gate_bank = [psA, psO]
            up_bank = [psB, psR]
            bank_free = [0, 0]
            hacc = 0
            hrow_free_g = [0, 0]
            hrow_free_v = [0, 0]
            p4_guards = S.allw()
            for ic in range(NI):
                hb = ic % 2
                for tb in range(NTB):
                    pp = tb % 2
                    pg, pu = gate_bank[pp], up_bank[pp]
                    wts = [("v", v_wu), ("v", bank_free[pp])]
                    if ic == 0 and tb == 0:
                        wts += list(v_xq_ab) + list(p4_guards)
                    for c in range(MC):
                        S.op("tensor", lambda E_, c_=c, i_=ic, t_=tb, p_=pg: E_.matmul(
                            p_[:], wreg(5632 * c_, I, BF16)[:, 128 * i_:128 * (i_ + 1)],
                            xnT[:, T * c_ + 512 * t_:T * c_ + 512 * (t_ + 1)],
                            start=(c_ == 0), stop=(c_ == MC - 1)),
                            waits=wts if c == 0 else ())
                    for c in range(MC):
                        tg = S.op("tensor", lambda E_, c_=c, i_=ic, t_=tb, p_=pu: E_.matmul(
                            p_[:], wreg(45056 + 5632 * c_, I, BF16)[:, 128 * i_:128 * (i_ + 1)],
                            xnT[:, T * c_ + 512 * t_:T * c_ + 512 * (t_ + 1)],
                            start=(c_ == 0), stop=(c_ == MC - 1)),
                            inc="t" if c == MC - 1 else None)
                    mmv = tg
                    S.op("scalar", lambda E_, p_=pg, i_=ic, b_=pp: E_.activation(
                        out=gsb[b_][:], in_=p_[:], func=AF.Silu,
                        scale=misc(GSC_OFF, 22, F32)[:, i_:i_ + 1]), waits=(("t", mmv),))
                    su = S.op("scalar", lambda E_, p_=pu, i_=ic, b_=pp: E_.activation(
                        out=usb[b_][:], in_=p_[:], func=AF.Copy,
                        scale=misc(USC_OFF, 22, F32)[:, i_:i_ + 1]), inc="s")
                    hwts = [("s", su)]
                    if tb == 0 and ic >= 2:
                        hwts += [(("ga" if hb == 0 else "gb"), hrow_free_g[hb]),
                                 ("vv", hrow_free_v[hb])]
                    bank_free[pp] = S.op("vector", lambda E_, b_=pp, hb_=hb, t_=tb: E_.tensor_tensor(
                        out=hrow[hb_][:, 512 * t_:512 * (t_ + 1)], in0=gsb[b_][:],
                        in1=usb[b_][:], op=ALU.mult), inc="v", waits=hwts)
                last_hv = bank_free[(NTB - 1) % 2]
                hrow_free_g[hb] = S.op("gpsimd", lambda E_, i_=ic, hb_=hb: E_.dma_start(
                    out=hsp[i_].rearrange("t p f -> p t f"),
                    in_=hrow[hb_][:].rearrange("p (t f) -> p t f", f=128)),
                    inc=("ga" if hb == 0 else "gb"), by=16, waits=(("v", last_hv),))
                hr = S.op("vector", lambda E_, hb_=hb: E_.reduce_max(
                    out=hmxt[:], in_=hrow[hb_][:], axis=AX.X, apply_absolute_value=True),
                    inc="vv", waits=(("v", last_hv), ("vv", hacc)))
                if ic == 0:
                    hacc = S.op("vector", lambda E_: E_.tensor_copy(
                        out=hmx[:], in_=hmxt[:]), inc="vv", waits=(("vv", hr),))
                else:
                    hacc = S.op("vector", lambda E_: E_.tensor_tensor(
                        out=hmx[:], in0=hmx[:], in1=hmxt[:], op=ALU.max), inc="vv",
                        waits=(("vv", hr),))
                hrow_free_v[hb] = hacc
            if "dbg_h" in dbg_outs:
                S.op("gpsimd", lambda E_: E_.dma_start(
                    out=dbg_outs["dbg_h"], in_=hsp), inc="gm", by=16,
                    waits=(("ga", S.val("ga")), ("gb", S.val("gb"))))
            gv = S.op("gpsimd", lambda E_: E_.dma_start(
                out=mxscr[0, 0:128], in_=hmx[:]), inc="gm", by=16,
                waits=(("v", S.val("v")), ("vv", hacc)))
            gv = S.op("gpsimd", lambda E_: E_.dma_start(
                out=row128[0:1, :], in_=mxscr[0:1, 0:128]), inc="gm", by=16,
                waits=(("gm", gv),))
            vmx = S.op("vector", lambda E_: E_.reduce_max(
                out=scslot(6), in_=row128[0:1, :], axis=AX.X), inc="v",
                waits=(("gm", gv),))
            gv = S.op("gpsimd", lambda E_: E_.dma_start(
                out=cc2i.ap(), in_=ones8f[:]), inc="gm", by=16, waits=(("gm", gv),))
            gv = S.op("gpsimd", lambda E_: E_.dma_start(
                out=cc2i.ap()[0, 0:1], in_=scslot(6)), inc="gm", by=16,
                waits=(("v", vmx), ("gm", gv)))
            arv = S.op("gpsimd", lambda E_: E_.collective_compute(
                "AllReduce", ALU.max, replica_groups=[list(range(8))],
                ins=[cc2i.ap().opt()], outs=[cc2o.ap().opt()]), inc="cc",
                waits=(("gm", gv),))
            gv = S.op("gpsimd", lambda E_: E_.dma_start(
                out=row8[0:1, :], in_=cc2o.ap()), inc="gm", by=16, waits=(("cc", arv),))
            v_hmg = S.op("vector", lambda E_: E_.tensor_copy(
                out=scslot(7), in_=row8[0:1, 0:1]), inc="v", waits=(("gm", gv),))

        v_attn = 0
        v_wd = 0
        v_fbc = 0
        fbc = wreg(61440, D, F32)
        if "p5" in _STAGES:
            wobf = wreg(45056, 8192, BF16)
            wost = act(4096, 4096, F32)
            p5_guards = S.allw()
            cw = 0
            for half in range(2):
                lv = dma("sync", "qa", wost[:],
                         wo_d.rearrange("p h m -> p (h m)")[:, 4096 * half:4096 * (half + 1)],
                         waits=((("v", cw),) if half else p5_guards))
                cw = S.op("vector", lambda E_, h_=half: E_.tensor_copy(
                    out=wobf[:, 4096 * h_:4096 * (h_ + 1)], in_=wost[:]), inc="v",
                    waits=(("qa", lv),))
            otb = [act(0, 1024, BF16), act(2048, 1024, BF16)]
            last_mm = [0, 0]
            prev_evac = 0
            for tt in range(NTT):
                b = tt % 2
                qs5 = "qa" if b == 0 else "qb"
                lv = dma("sync", qs5, otb[b][:].rearrange("p (h f) -> p h f", f=128),
                         otsp[:, tt].rearrange("h p f -> p h f"),
                         waits=(("t", last_mm[b]), ("go", p2_ot_g)) + ((("v", cw),) if tt == 0 else ()))
                for h in range(H):
                    for half in range(2):
                        pw = psQ1 if half == 0 else psQ2
                        wts = []
                        if h == 0 and half == 0:
                            wts = [(qs5, lv), ("v", cw), ("v", prev_evac)]
                        tv = S.op("tensor", lambda E_, h_=h, f_=half, b_=b, p_=pw: E_.matmul(
                            p_[:], otb[b_][:, 128 * h_:128 * (h_ + 1)],
                            wobf[:, D * h_ + 512 * f_:D * h_ + 512 * (f_ + 1)],
                            start=(h_ == 0), stop=(h_ == H - 1)),
                            inc="t" if (h == H - 1 and half == 1) else None,
                            waits=wts if (h == 0 and half == 0) else ())
                last_mm[b] = tv
                for half in range(2):
                    pw = psQ1 if half == 0 else psQ2
                    prev_evac = S.op("vector", lambda E_, t_=tt, f_=half, p_=pw: E_.tensor_copy(
                        out=big(0, NTT * D)[:, D * t_ + 512 * f_:D * t_ + 512 * (f_ + 1)],
                        in_=p_[:]), inc="v", waits=(("t", last_mm[b]),))
            v_attn = prev_evac
            if "dbg_attn" in dbg_outs:
                S.op("gpsimd", lambda E_: E_.dma_start(
                    out=dbg_outs["dbg_attn"], in_=big(0, NTT * D)[:]), inc="gm", by=16,
                    waits=(("v", v_attn),))

            rmcol8 = misc(RM_OFF, 8, F32)
            wdst = act(4096, I, F32)
            lastv = v_attn
            for c in range(MC):
                lv = dma("sync", "qa", wdst[:], wdn_d[128 * c:128 * (c + 1), :],
                         waits=(("v", lastv),))
                lastv = S.op("vector", lambda E_, c_=c: E_.reduce_max(
                    out=rmcol8[:, c_:c_ + 1], in_=wdst[:], axis=AX.X,
                    apply_absolute_value=True), inc="v", waits=(("qa", lv),))
            c3 = S.op("vector", lambda E_: E_.tensor_scalar_mul(
                out=misc(WDSC_OFF, 8, F32)[:], in0=rmcol8[:], scalar1=1.0 / 127.0),
                inc="vv", waits=(("v", lastv),))
            c3 = S.op("vector", lambda E_: E_.reciprocal(out=rmcol8[:], in_=rmcol8[:]),
                      inc="vv", waits=(("vv", c3),))
            vv = S.op("vector", lambda E_: E_.tensor_scalar_mul(
                out=rmcol8[:], in0=rmcol8[:], scalar1=127.0), inc="v", waits=(("vv", c3),))
            gl = 0
            for c in range(MC):
                gl = S.op("gpsimd", lambda E_, c_=c: E_.dma_start(
                    out=wscr[0, 128 * c_:128 * (c_ + 1)], in_=rmcol8[:, c_:c_ + 1]),
                    inc="gm", by=16, waits=(("v", vv),) if c == 0 else ())
            invrow2 = rowap(0, D, F32)
            gl2 = S.op("gpsimd", lambda E_: E_.dma_start(
                out=invrow2[0:1, :], in_=wscr[0:1, 0:D]), inc="gm", by=16,
                waits=(("gm", gl), ("t", S.val("t"))))
            inv_bc2 = act(4096 + 11264, D, F32)
            lastbc = 0
            for j in range(2):
                tbv = S.op("tensor", lambda E_, j_=j: E_.matmul(
                    psBC[:], onesrowf[:], invrow2[0:1, 512 * j_:512 * (j_ + 1)],
                    start=True, stop=True), inc="t",
                    waits=(("gm", gl2), ("v", lastbc if j else v_attn), ("s", S.val("s"))))
                lastbc = S.op("vector", lambda E_, j_=j: E_.tensor_copy(
                    out=inv_bc2[:, 512 * j_:512 * (j_ + 1)], in_=psBC[:]), inc="v",
                    waits=(("t", tbv),))
            lastq = 0
            for ic in range(NI):
                lv = dma("sync", "qb", wdst[:, 0:D], wdt_d[128 * ic:128 * (ic + 1), :],
                         waits=(("v", lastq if lastq else lastbc),))
                q5 = S.op("vector", lambda E_: E_.scalar_tensor_tensor(
                    out=wdst[:, 0:D], in0=wdst[:, 0:D], scalar=1.0, in1=inv_bc2[:],
                    op0=ALU.mult, op1=ALU.mult), inc="vv",
                    waits=(("qb", lv), ("v", lastbc)))
                lastq = S.op("vector", lambda E_, i_=ic: E_.tensor_scalar(
                    out=wreg(2048 * i_, D, BF16)[:], in0=wdst[:, 0:D], scalar1=MAGIC,
                    scalar2=-MAGIC, op0=ALU.add, op1=ALU.add), inc="v",
                    waits=(("vv", q5),))
            v_wd = lastq
            if "dbg_dq" in dbg_outs:
                S.op("gpsimd", lambda E_: E_.dma_start(
                    out=dbg_outs["dbg_dq"], in_=wreg(0, NI * D, BF16)[:]), inc="gm",
                    by=16, waits=(("v", v_wd),))

            lcd = dma("sync", "qc", scslot(8), clipd_d)
            tb1 = S.op("tensor", lambda E_: E_.matmul(
                psBC[:, 0:2], onesrowf[:], scal[0:1, 7:9], start=True, stop=True),
                inc="t", waits=(("qc", lcd), ("v", v_hmg), ("v", lastbc)))
            c128b = sb_c128
            c4 = S.op("vector", lambda E_: E_.tensor_copy(
                out=c128b[:, 4:6], in_=psBC[:, 0:2]), inc="vv", waits=(("t", tb1),))
            sv = S.op("scalar", lambda E_: E_.activation(
                out=c128b[:, 6:7], in_=c128b[:, 5:6], func=AF.Exp, scale=-1.0), inc="s",
                waits=(("vv", c4),))
            c4 = S.op("vector", lambda E_: E_.tensor_scalar_add(
                out=c128b[:, 6:7], in0=c128b[:, 6:7], scalar1=1.0), inc="vv",
                waits=(("s", sv),))
            c4 = S.op("vector", lambda E_: E_.reciprocal(
                out=c128b[:, 7:8], in_=c128b[:, 6:7]), inc="vv", waits=(("vv", c4),))
            c4 = S.op("vector", lambda E_: E_.tensor_tensor(
                out=c128b[:, 7:8], in0=c128b[:, 7:8], in1=c128b[:, 4:5], op=ALU.mult),
                inc="vv", waits=(("vv", c4),))
            c4 = S.op("vector", lambda E_: E_.tensor_scalar_mul(
                out=s2pair[:, 0:1], in0=c128b[:, 7:8], scalar1=1.0 / 127.0), inc="vv",
                waits=(("vv", c4),))
            v_inv2 = S.op("vector", lambda E_: E_.reciprocal(
                out=s2pair[:, 1:2], in_=s2pair[:, 0:1]), inc="v", waits=(("vv", c4),))
            vv = S.op("vector", lambda E_: E_.tensor_single_scalar(
                out=misc(WDSC_OFF, 8, F32)[:], in_=misc(WDSC_OFF, 8, F32)[:],
                scalar=s2pair[:, 0:1], op=ALU.mult), inc="v", waits=(("v", v_inv2),))
            gl = 0
            for c in range(MC):
                gl = S.op("gpsimd", lambda E_, c_=c: E_.dma_start(
                    out=wscr[0, 2048 + 128 * c_:2048 + 128 * (c_ + 1)],
                    in_=misc(WDSC_OFF, 8, F32)[:, c_:c_ + 1]), inc="gm", by=16,
                    waits=(("v", vv),) if c == 0 else ())
            fbrow = rowap(0, D, F32, byteoff=4096)
            gl2 = S.op("gpsimd", lambda E_: E_.dma_start(
                out=fbrow[0:1, :], in_=wscr[0:1, 2048:2048 + D]), inc="gm", by=16,
                waits=(("gm", gl),))
            lastf = 0
            for j in range(2):
                tbv = S.op("tensor", lambda E_, j_=j: E_.matmul(
                    psBC[:], onesrowf[:], fbrow[0:1, 512 * j_:512 * (j_ + 1)],
                    start=True, stop=True), inc="t",
                    waits=(("gm", gl2), ("v", lastf if j else v_inv2), ("s", S.val("s"))))
                lastf = S.op("vector", lambda E_, j_=j: E_.tensor_copy(
                    out=fbc[:, 512 * j_:512 * (j_ + 1)], in_=psBC[:]), inc="v",
                    waits=(("t", tbv),))
            v_fbc = lastf

        if "p6" in _STAGES:
            hq = [act(0, I, BF16), act(5632, I, BF16)]
            h16in = [act(11264, I, FP16), act(16896, I, FP16)]
            qscr = wreg(45056, I, F32)
            xrld = [wreg(65536, D, F32), wreg(69632, D, F32)]
            osum = [wreg(73728, D, F32), wreg(77824, D, F32)]
            last_sq = [0, 0]
            last_hq_t = [0, 0]
            last_e3 = [0, 0]
            last_st = [0, 0]
            prev_vq2 = 0
            prev_evac6 = 0
            p6_guards = S.allw()
            for tt in range(NTT):
                b = tt % 2
                qs6 = "qa" if b == 0 else "qb"
                qx6 = "qc" if b == 0 else "qd"
                os6 = "outa" if b == 0 else "outb"
                wts = [("s", last_sq[b])]
                if tt == 0:
                    wts += list(p6_guards)
                lh = dma("sync", qs6, h16in[b][:].rearrange("p (i f) -> p i f", f=128),
                         hsp[:, tt].rearrange("i p f -> p i f"), waits=wts)
                lx = dma("sync", qx6, xrld[b][:], x_d[128 * tt:128 * (tt + 1), :],
                         waits=(("v", last_e3[b]),))
                sq1 = S.op("scalar", lambda E_, b_=b: E_.activation(
                    out=qscr[:], in_=h16in[b_][:], func=AF.Identity, scale=inv2vec,
                    bias=consts[:, 0:1]), inc="s",
                    waits=((qs6, lh), ("v", v_inv2), ("v", prev_vq2)))
                last_sq[b] = sq1
                vq1 = S.op("vector", lambda E_: E_.tensor_scalar(
                    out=qscr[:], in0=qscr[:], scalar1=-MAGIC, scalar2=127.0,
                    op0=ALU.add, op1=ALU.min), inc="vv", waits=(("s", sq1),))
                vq2 = S.op("vector", lambda E_, b_=b: E_.tensor_scalar_max(
                    out=hq[b_][:], in0=qscr[:], scalar1=-128.0), inc="v",
                    waits=(("t", last_hq_t[b]), ("vv", vq1)))
                prev_vq2 = vq2
                for ic in range(NI):
                    for half in range(2):
                        pd = psQ1 if half == 0 else psQ2
                        wts2 = []
                        if ic == 0 and half == 0:
                            wts2 = [("v", vq2), ("v", v_wd), ("v", v_fbc),
                                    ("v", prev_evac6)]
                        tv = S.op("tensor", lambda E_, i_=ic, f_=half, b_=b, p_=pd: E_.matmul(
                            p_[:], hq[b_][:, 128 * i_:128 * (i_ + 1)],
                            wreg(2048 * i_, D, BF16)[:, 512 * f_:512 * (f_ + 1)],
                            start=(i_ == 0), stop=(i_ == NI - 1)),
                            inc="t" if (ic == NI - 1 and half == 1) else None,
                            waits=wts2)
                last_hq_t[b] = tv
                for half in range(2):
                    pd = psQ1 if half == 0 else psQ2
                    e1 = S.op("vector", lambda E_, f_=half, b_=b, p_=pd: E_.scalar_tensor_tensor(
                        out=osum[b_][:, 512 * f_:512 * (f_ + 1)], in0=p_[:], scalar=1.0,
                        in1=fbc[:, 512 * f_:512 * (f_ + 1)], op0=ALU.mult, op1=ALU.mult),
                        inc="vv", waits=(("t", tv), (qx6, lx), (os6, last_st[b])))
                    e2 = S.op("vector", lambda E_, f_=half, b_=b, t_=tt: E_.tensor_tensor(
                        out=osum[b_][:, 512 * f_:512 * (f_ + 1)],
                        in0=osum[b_][:, 512 * f_:512 * (f_ + 1)],
                        in1=big(0, NTT * D)[:, D * t_ + 512 * f_:D * t_ + 512 * (f_ + 1)],
                        op=ALU.add), inc="vv", waits=(("vv", e1),))
                    prev_evac6 = S.op("vector", lambda E_, f_=half, b_=b: E_.tensor_tensor(
                        out=osum[b_][:, 512 * f_:512 * (f_ + 1)],
                        in0=osum[b_][:, 512 * f_:512 * (f_ + 1)],
                        in1=xrld[b_][:, 512 * f_:512 * (f_ + 1)], op=ALU.add), inc="v",
                        waits=(("vv", e2),))
                last_e3[b] = prev_evac6
                last_st[b] = S.op("gpsimd", lambda E_, b_=b, t_=tt: E_.dma_start(
                    out=out_d[128 * t_:128 * (t_ + 1), :], in_=osum[b_][:]),
                    inc=os6, by=16, waits=(("v", prev_evac6),))
            S.wait("gpsimd", ("outa", S.val("outa")), ("outb", S.val("outb")))
            S.wait("sync", ("outa", S.val("outa")), ("outb", S.val("outb")))

        S.emit(block, sems)

    return nc



def _prep_core_inputs(b, ins):
    x = np.ascontiguousarray(np.asarray(ins["L0_attn_x"])[b], dtype=np.float32)
    wq = np.asarray(ins["L0_attn_wq"], dtype=np.float32)
    wk = np.asarray(ins["L0_attn_wk"], dtype=np.float32)
    wv = np.asarray(ins["L0_attn_wv"], dtype=np.float32).reshape(D, D)
    wo = np.ascontiguousarray(np.asarray(ins["L0_attn_wo"], dtype=np.float32))
    wg = np.asarray(ins["L0_ffn_w_gate"], dtype=np.float32)
    wu = np.asarray(ins["L0_ffn_w_up"], dtype=np.float32)
    wd = np.asarray(ins["L0_ffn_w_down"], dtype=np.float32)

    def qk_r(w):
        return w.reshape(MC, 128, H, DH).transpose(2, 1, 0, 3)

    wqk = np.ascontiguousarray(np.stack([qk_r(wq), qk_r(wk)]))
    wv_r = np.ascontiguousarray(wv.reshape(MC, 128, 2, 512).transpose(2, 1, 0, 3))

    pos = np.arange(T, dtype=np.float64)
    half = DH // 2
    freqs = np.power(10000.0, -np.arange(half, dtype=np.float64) / half)
    rad = pos[None, :] * freqs[:, None]
    cos = np.cos(rad)
    sin = np.sin(rad)
    tabC = np.concatenate([cos, cos], axis=0).astype(ml_dtypes.bfloat16)
    tabS = np.concatenate([-sin, sin], axis=0).astype(ml_dtypes.bfloat16)

    kk = np.arange(128)[:, None]
    qq = np.arange(512)[None, :]
    m = [(qq >= kk + 128 * j).astype(np.float32) for j in range(4)]
    masksv = np.concatenate(m, axis=1).astype(ml_dtypes.bfloat16)

    consts = np.zeros((128, 4), dtype=np.float32)
    consts[:, 0] = MAGIC
    consts[:, 1] = EPS_LN
    consts[:, 2] = EPS_RMS

    return dict(
        x=x,
        wqk=wqk,
        wv=wv_r,
        wo=np.ascontiguousarray(wo.transpose(1, 0, 2)),
        wgn=np.ascontiguousarray(wg),
        wgt=np.ascontiguousarray(wg.T),
        wun=np.ascontiguousarray(wu),
        wut=np.ascontiguousarray(wu.T),
        wdn=np.ascontiguousarray(wd),
        wdt=np.ascontiguousarray(wd.T),
        clipx=np.asarray(ins["L0_ffn_clip_x"], dtype=np.float32).reshape(1, 1),
        clipd=np.asarray(ins["L0_ffn_clip_down"], dtype=np.float32).reshape(1, 1),
        tabC=np.ascontiguousarray(tabC),
        tabS=np.ascontiguousarray(tabS),
        masks=np.ascontiguousarray(masksv),
        ident=np.eye(128, dtype=ml_dtypes.bfloat16),
        swapm=np.ascontiguousarray(np.roll(np.eye(128), 64, axis=0)).astype(ml_dtypes.bfloat16),
        onescol=np.ones((128, 1), dtype=ml_dtypes.bfloat16),
        onesrow=np.ones((1, 128), dtype=ml_dtypes.bfloat16),
        onesrowf=np.ones((1, 128), dtype=np.float32),
        ones8f=np.ones((1, 8), dtype=np.float32),
        consts=consts,
    )


def kernel(**inputs):
    nc = build_program(dbg=_DBG.get("dump", ()))
    in_maps = [_prep_core_inputs(b, inputs) for b in range(8)]
    res = run_bass_kernel_spmd(nc, in_maps, core_ids=list(range(8)))
    if _DBG.get("dump"):
        _DBG["results"] = res.results
    if "p6" not in _STAGES:
        return np.zeros((8, T, D), dtype=np.float32)
    out = np.stack([np.asarray(res.results[b]["out"]) for b in range(8)], axis=0)
    return out.astype(np.float32)


# version 04 (final)
# speedup vs baseline: 1.1316x; 1.1316x over previous
from contextlib import ExitStack

import numpy as np
import ml_dtypes

import concourse.bass as bass
import concourse.mybir as mybir
from concourse.bass_utils import run_bass_kernel_spmd

F32 = mybir.dt.float32
BF16 = mybir.dt.bfloat16
FP16 = mybir.dt.float16
U8 = mybir.dt.uint8
AF = mybir.ActivationFunctionType
ALU = mybir.AluOpType
AX = mybir.AxisListType

MAGIC = 12582912.0
T = 4096
D = 1024
H = 8
DH = 128
I = 2816
NI = I // 128
MC = D // 128
NTT = T // 128
NTB = T // 512
EPS_LN = 1e-5
EPS_RMS = 1e-6

ENGS = ("sync", "vector", "scalar", "tensor", "gpsimd")
SEMS = ("q", "qa", "qb", "qc", "qd", "q2a", "q2b", "ga", "gb", "gm", "gr", "go",
        "outa", "outb", "cc", "t", "v", "s", "vv", "ss")

_STAGES = {"p1", "p1t", "ar1", "p2", "p3", "p35", "p4", "p5", "p6"}
_DBG = {}


class Sched:
    def __init__(self):
        self.ops = {e: [] for e in ENGS}
        self.cnt = {}

    def val(self, sem):
        return self.cnt.get(sem, 0)

    def op(self, eng, fn, inc=None, by=1, waits=()):
        v = None
        if inc is not None:
            v = self.cnt.get(inc, 0) + by
            self.cnt[inc] = v
        self.ops[eng].append((fn, inc, by, tuple(waits)))
        return v

    def wait(self, eng, *deps):
        self.ops[eng].append((None, None, 0, tuple(deps)))

    def allw(self, *skip):
        return tuple((s, self.val(s)) for s in SEMS if s not in skip and self.val(s) > 0)

    def barrier(self):
        w = self.allw()
        for e in ENGS:
            self.wait(e, *w)

    def emit(self, block, semh):
        def mk_body(eng_name):
            ops = self.ops[eng_name]

            def body(E):
                for fn, inc, by, waits in ops:
                    for s, v in waits:
                        if v > 0:
                            E.wait_ge(semh[s], v)
                    if fn is not None:
                        ins = fn(E)
                        if inc is not None:
                            ins.then_inc(semh[inc], by)
            return body

        block.sync(mk_body("sync"))
        block.vector(mk_body("vector"))
        block.scalar(mk_body("scalar"))
        block.tensor(mk_body("tensor"))
        block.gpsimd(mk_body("gpsimd"))


def build_program(dbg=()):
    nc = bass.Bass("TRN2", target_bir_lowering=False, debug=False)
    dbg = set(dbg)

    def din(name, shape, dt):
        return nc.dram_tensor(name, list(shape), dt, kind="ExternalInput").ap()

    x_d = din("x", (T, D), F32)
    wqk_d = din("wqk", (2, H, 128, MC, 128), F32)
    wv_d = din("wv", (2, 128, MC, 512), F32)
    wo_d = din("wo", (128, H, D), F32)
    wgn_d = din("wgn", (I, D), F32)
    wgt_d = din("wgt", (D, I), F32)
    wun_d = din("wun", (I, D), F32)
    wut_d = din("wut", (D, I), F32)
    wdn_d = din("wdn", (D, I), F32)
    wdt_d = din("wdt", (I, D), F32)
    clipx_d = din("clipx", (1, 1), F32)
    clipd_d = din("clipd", (1, 1), F32)
    tabC_d = din("tabC", (128, T), BF16)
    tabS_d = din("tabS", (128, T), BF16)
    masks_d = din("masks", (128, 2048), BF16)
    ident_d = din("ident", (128, 128), BF16)
    swapm_d = din("swapm", (128, 128), BF16)
    onescol_d = din("onescol", (128, 1), BF16)
    onesrow_d = din("onesrow", (1, 128), BF16)
    onesrowf_d = din("onesrowf", (1, 128), F32)
    ones8f_d = din("ones8f", (1, 8), F32)
    consts_d = din("consts", (128, 4), F32)

    out_d = nc.dram_tensor("out", [T, D], F32, kind="ExternalOutput").ap()
    xnt_in_d = din("xnt_in", (128, MC * T), BF16) if _DBG.get("fakexnt") else None

    xnsp = nc.dram_tensor("xnsp", [T, D], F32, kind="Internal").ap()
    hsp = nc.dram_tensor("hsp", [NI, NTT, 128, 128], FP16, kind="Internal").ap()
    otsp = nc.dram_tensor("otsp", [H, NTT, 128, 128], BF16, kind="Internal").ap()
    rowscr = nc.dram_tensor("rowscr", [2, T], BF16, kind="Internal").ap()
    rowscr2 = nc.dram_tensor("rowscr2", [2, T], BF16, kind="Internal").ap()
    wscr = nc.dram_tensor("wscr", [1, 4096], F32, kind="Internal").ap()
    mxscr = nc.dram_tensor("mxscr", [1, 128], F32, kind="Internal").ap()
    cc1i = nc.dram_tensor("cc1i", [1, 8], F32, kind="Internal")
    cc1o = nc.dram_tensor("cc1o", [1, 8], F32, kind="Internal")
    cc2i = nc.dram_tensor("cc2i", [1, 8], F32, kind="Internal")
    cc2o = nc.dram_tensor("cc2o", [1, 8], F32, kind="Internal")

    dbg_outs = {}
    for name, shape, dt in (
        ("dbg_xnt", (128, MC * T), BF16),
        ("dbg_qt", (2, 128, T), BF16),
        ("dbg_v", (128, T), BF16),
        ("dbg_ot", (H, NTT, 128, 128), BF16),
        ("dbg_attn", (128, NTT * D), BF16),
        ("dbg_xqt", (128, MC * T), BF16),
        ("dbg_gq", (128, MC * I), BF16),
        ("dbg_uq", (128, MC * I), BF16),
        ("dbg_dq", (128, NI * D), BF16),
        ("dbg_h", (NI, NTT, 128, 128), FP16),
        ("dbg_sc", (1, 16), F32),
    ):
        if name in dbg:
            dbg_outs[name] = nc.dram_tensor(name, list(shape), dt, kind="ExternalOutput").ap()

    S = Sched()

    with ExitStack() as stk:
        E = stk.enter_context
        A_BIG = E(nc.sbuf_tensor("A_BIG", [128, 65536], U8))
        A_W = E(nc.sbuf_tensor("A_W", [128, 90112], U8))
        A_ACT = E(nc.sbuf_tensor("A_ACT", [128, 22528], U8))
        A_MISC = E(nc.sbuf_tensor("A_MISC", [128, 12064], U8))
        sb_sc = E(nc.sbuf_tensor("sb_sc", [1, 16], F32))
        sb_r128 = E(nc.sbuf_tensor("sb_r128", [1, 128], F32))
        sb_r8 = E(nc.sbuf_tensor("sb_r8", [1, 8], F32))
        sb_c128 = E(nc.sbuf_tensor("sb_c128", [128, 8], F32))
        tabC = E(nc.sbuf_tensor("sb_tabC", [128, T], BF16))
        tabS = E(nc.sbuf_tensor("sb_tabS", [128, T], BF16))
        masks = E(nc.sbuf_tensor("sb_masks", [128, 2048], BF16))
        ident = E(nc.sbuf_tensor("sb_ident", [128, 128], BF16))
        swapm = E(nc.sbuf_tensor("sb_swapm", [128, 128], BF16))
        onescol = E(nc.sbuf_tensor("sb_onescol", [128, 1], BF16))
        onesrow = E(nc.sbuf_tensor("sb_onesrow", [1, 128], BF16))
        onesrowf = E(nc.sbuf_tensor("sb_onesrowf", [1, 128], F32))
        ones8f = E(nc.sbuf_tensor("sb_ones8f", [1, 8], F32))
        consts = E(nc.sbuf_tensor("sb_consts", [128, 4], F32))

        psS1 = E(nc.psum_tensor("psS1", [128, 1024], F32))
        psS2 = E(nc.psum_tensor("psS2", [128, 1024], F32))
        psO = E(nc.psum_tensor("psO", [128, 512], F32))
        psR = E(nc.psum_tensor("psR", [128, 512], F32))
        psBC = E(nc.psum_tensor("psBC", [128, 512], F32))
        psMS = E(nc.psum_tensor("psMS", [128, 512], F32))
        psA = psS1[:, 0:512]
        psB = psS1[:, 512:1024]
        psQ1 = psS2[:, 0:512]
        psQ2 = psS2[:, 512:1024]

        sems = {}
        for sname in SEMS:
            sems[sname] = E(nc.semaphore(sname))
        block = E(nc.Block())

        def bview(arena, off, nelem, dt):
            size = {F32: 4, BF16: 2, FP16: 2}[dt]
            return arena[:, off:off + nelem * size].bitcast(dt)

        def big(off, n, dt=BF16):
            return bview(A_BIG, off, n, dt)

        def wreg(off, n, dt):
            return bview(A_W, off, n, dt)

        def act(off, n, dt):
            return bview(A_ACT, off, n, dt)

        def misc(off, n, dt):
            return bview(A_MISC, off, n, dt)

        RM_OFF, GSC_OFF, USC_OFF, WDSC_OFF = 11264, 11352, 11440, 11528
        ST_OFF, SC_OFF = 11584, 11600
        MSQ_OFF, MSQB_OFF = 11712, 11840
        INV1_OFF, INV2_OFF = 11968, 12032
        HMX_OFF, HMXT_OFF = 11904, 11936
        R128_OFF, R8_OFF = 12224, 12736

        stats = misc(ST_OFF, 4, F32)
        scal = sb_sc

        def scslot(i):
            return scal[0:1, i:i + 1]

        def rowap(part, cols, dt, byteoff=0):
            size = {F32: 4, BF16: 2, FP16: 2}[dt]
            return A_MISC[part:part + 1, byteoff:byteoff + cols * size].bitcast(dt)

        s1pair = misc(INV1_OFF, 2, F32)
        s2pair = misc(INV2_OFF, 2, F32)
        inv1vec = s1pair[:, 1:2]
        inv2vec = s2pair[:, 1:2]
        msq = misc(MSQ_OFF, 32, F32)
        msqb = misc(MSQB_OFF, 32, BF16)
        hmx = misc(HMX_OFF, 1, F32)
        hmxt = misc(HMXT_OFF, 1, F32)
        row128 = sb_r128
        row8 = sb_r8

        def dma(eng, sem, out, in_, waits=()):
            return S.op(eng, lambda E_, o=out, i=in_: E_.dma_start(out=o, in_=i),
                        inc=sem, by=16, waits=waits)

        for (t_, d_) in ((tabC, tabC_d), (tabS, tabS_d), (masks, masks_d), (ident, ident_d),
                         (swapm, swapm_d),
                         (onescol, onescol_d), (onesrow, onesrow_d), (onesrowf, onesrowf_d),
                         (ones8f, ones8f_d), (consts, consts_d)):
            dma("sync", "q", t_[:], d_)
        v_consts = 16 * 10

        xnT = big(0, MC * T)
        if _DBG.get("fakexnt"):
            dma("sync", "q", xnT[:], xnt_in_d)

        xtile = [act(0, D, F32), act(4096, D, F32)]
        xn32 = [act(8192, D, F32), act(12288, D, F32)]
        xnbf = [act(16384, D, BF16), act(18432, D, BF16)]
        mxacc = misc(11600, 1, F32)
        mxtmp = misc(11632, 1, F32)

        v_mxg = 0
        if "p1" in _STAGES:
            last_sp = [0, 0]
            last_stats_v = 0
            macc = 0
            last_cb = [0, 0]
            last_tp = [0, 0]
            last_ln = [0, 0]
            tguard = 0
            for tt in range(NTT):
                b = tt % 2
                qs = "qa" if b == 0 else "qb"
                gs = "ga" if b == 0 else "gb"
                lv = dma("sync", qs, xtile[b][:], x_d[128 * tt:128 * (tt + 1), :],
                         waits=(("s", last_ln[b]),))
                r1 = S.op("vector", lambda E_, xb=xtile[b], q_=qs: E_.reduce_sum(
                    out=stats[:, 0:1], in_=xb[:], axis=AX.X), inc="vv",
                    waits=((qs, lv), ("v", last_stats_v)))
                sv = S.op("scalar", lambda E_, xb=xtile[b], du=xn32[b]: E_.activation(
                    out=du[:], in_=xb[:], func=AF.Square, accum_out=stats[:, 1:2]),
                    inc="s", waits=((qs, lv), (gs, last_sp[b]), ("v", last_cb[b]),
                                    ("v", last_stats_v)))
                r2 = S.op("vector", lambda E_: E_.tensor_scalar_mul(
                    out=stats[:, 0:1], in0=stats[:, 0:1], scalar1=1.0 / D), inc="vv",
                    waits=(("s", sv), ("vv", r1)))
                r3 = S.op("vector", lambda E_: E_.tensor_tensor(
                    out=stats[:, 2:3], in0=stats[:, 0:1], in1=stats[:, 0:1], op=ALU.mult),
                    inc="vv", waits=(("vv", r2),))
                vv = S.op("vector", lambda E_: E_.scalar_tensor_tensor(
                    out=stats[:, 1:2], in0=stats[:, 1:2], scalar=1.0 / D, in1=stats[:, 2:3],
                    op0=ALU.mult, op1=ALU.subtract), inc="v", waits=(("vv", r3),))
                l1 = S.op("scalar", lambda E_: E_.activation(
                    out=stats[:, 2:3], in_=stats[:, 1:2], func=AF.Ln, bias=consts[:, 1:2]),
                    inc="ss", waits=(("v", vv), ("q", v_consts)))
                sv = S.op("scalar", lambda E_: E_.activation(
                    out=stats[:, 2:3], in_=stats[:, 2:3], func=AF.Exp, scale=-0.5), inc="s",
                    waits=(("ss", l1),))
                vv = S.op("vector", lambda E_: E_.scalar_tensor_tensor(
                    out=stats[:, 3:4], in0=stats[:, 0:1], scalar=-1.0, in1=stats[:, 2:3],
                    op0=ALU.mult, op1=ALU.mult), inc="v", waits=(("s", sv),))
                last_stats_v = vv
                lnv = S.op("scalar", lambda E_, xb=xtile[b], ob=xn32[b]: E_.activation(
                    out=ob[:], in_=xb[:], func=AF.Identity, scale=stats[:, 2:3],
                    bias=stats[:, 3:4]), inc="s", waits=(("v", vv),))
                last_ln[b] = lnv
                last_sp[b] = S.op("gpsimd", lambda E_, ob=xn32[b], t=tt: E_.dma_start(
                    out=xnsp[128 * t:128 * (t + 1), :], in_=ob[:]), inc=gs, by=16,
                    waits=(("s", lnv),))
                ra = S.op("vector", lambda E_, ob=xn32[b]: E_.reduce_max(
                    out=mxtmp[:], in_=ob[:], axis=AX.X, apply_absolute_value=True),
                    inc="vv", waits=(("s", lnv), ("vv", macc)))
                if tt == 0:
                    macc = S.op("vector", lambda E_: E_.tensor_copy(
                        out=mxacc[:], in_=mxtmp[:]), inc="vv", waits=(("vv", ra),))
                else:
                    macc = S.op("vector", lambda E_: E_.tensor_tensor(
                        out=mxacc[:], in0=mxacc[:], in1=mxtmp[:], op=ALU.max),
                        inc="vv", waits=(("vv", ra),))
                if "p1t" in _STAGES:
                    cbv = S.op("vector", lambda E_, ob=xn32[b], bb=xnbf[b]: E_.tensor_copy(
                        out=bb[:], in_=ob[:]), inc="v", waits=(("t", last_tp[b]), ("s", lnv)))
                    last_cb[b] = cbv
                    for c in range(MC):
                        pq = psQ1 if c % 2 == 0 else psQ2
                        tv = S.op("tensor", lambda E_, bb=xnbf[b], cc=c, p_=pq: E_.matmul(
                            p_[:, 0:128], bb[:, 128 * cc:128 * (cc + 1)], ident[:],
                            start=True, stop=True), inc="t",
                            waits=(("v", cbv), ("v", tguard), ("q", v_consts)))
                        tguard = S.op("vector", lambda E_, cc=c, t=tt, p_=pq: E_.tensor_copy(
                            out=xnT[:, T * cc + 128 * t:T * cc + 128 * (t + 1)],
                            in_=p_[:, 0:128]), inc="v", waits=(("t", tv),))
                    last_tp[b] = S.val("t")

            v_mx_last = S.val("v")
            vv_mx_last = macc
        if "ar1" in _STAGES:
            gv = S.op("gpsimd", lambda E_: E_.dma_start(
                out=mxscr[0, 0:128], in_=mxacc[:]), inc="gm", by=16,
                waits=(("v", v_mx_last), ("vv", vv_mx_last)))
            gv = S.op("gpsimd", lambda E_: E_.dma_start(
                out=row128[0:1, :], in_=mxscr[0:1, 0:128]), inc="gm", by=16,
                waits=(("gm", gv),))
            vmx = S.op("vector", lambda E_: E_.reduce_max(
                out=scslot(0), in_=row128[0:1, :], axis=AX.X), inc="v",
                waits=(("gm", gv),))
            gv = S.op("gpsimd", lambda E_: E_.dma_start(
                out=cc1i.ap(), in_=ones8f[:]), inc="gm", by=16,
                waits=(("q", v_consts), ("gm", gv)))
            gv = S.op("gpsimd", lambda E_: E_.dma_start(
                out=cc1i.ap()[0, 0:1], in_=scslot(0)), inc="gm", by=16,
                waits=(("v", vmx), ("gm", gv)))
            arv = S.op("gpsimd", lambda E_: E_.collective_compute(
                "AllReduce", ALU.max, replica_groups=[list(range(8))],
                ins=[cc1i.ap().opt()], outs=[cc1o.ap().opt()]), inc="cc",
                waits=(("gm", gv),))
            gv = S.op("gpsimd", lambda E_: E_.dma_start(
                out=row8[0:1, :], in_=cc1o.ap()), inc="gm", by=16, waits=(("cc", arv),))
            v_mxg = S.op("vector", lambda E_: E_.tensor_copy(
                out=scslot(1), in_=row8[0:1, 0:1]), inc="v", waits=(("gm", gv),))

        if "p1" in _STAGES and "dbg_xnt" in dbg_outs:
            S.op("gpsimd", lambda E_: E_.dma_start(
                out=dbg_outs["dbg_xnt"], in_=xnT[:]), inc="gm", by=16,
                waits=(("v", S.val("v")), ("t", S.val("t"))))

        VSLOT = [wreg(8192 * i, T, BF16) for i in range(5)]
        wstage = wreg(40960, 4096, F32)
        wqh = [wreg(57344 + 2048 * i, D, BF16) for i in range(2)]
        wkh = [wreg(61440 + 2048 * i, D, BF16) for i in range(2)]
        m1b = [wreg(65536 + 2048 * i, 512, F32) for i in range(2)]
        wvb = wreg(69632, 4096, BF16)

        qTb = act(0, T, BF16)
        kTb = act(8192, T, BF16)
        PTD = [act(16384 + 2048 * i, 1024, BF16) for i in range(2)]
        binv = act(21504, 512, BF16)
        obuf = act(20480, 512, BF16)
        sqb = wreg(88064, 512, BF16)
        qsw = wreg(77824, D, BF16)

        p2_ot_g = 0
        if "p2" in _STAGES:
            vslot_evac = [0] * 5
            stage_free_v = 0
            psq_guard_v = 0
            prev_obuf_v = 0
            prev_obuf_g = 0
            last_s2v = 0
            sfree = [0, 0]
            PT_free = [0, 0]
            prev_vv2 = 0
            prev_rope = 0
            psw_guard = [0, 0]
            last_g1 = 0
            w_guard = [0, 0]
            v_p1_all = S.allw()

            def v_batch(b4):
                nonlocal stage_free_v, psq_guard_v
                lv = dma("sync", "qa", wstage[:].rearrange("p (c j) -> p c j", j=512), wv_d[b4],
                         waits=(("v", stage_free_v),) + (v_p1_all if b4 == 0 else ()))
                cv = S.op("vector", lambda E_: E_.tensor_copy(
                    out=wvb[:], in_=wstage[:]), inc="v", waits=(("qa", lv),))
                stage_free_v = cv
                last_evac = cv
                for tt in range(NTT):
                    wts = [("v", last_evac)]
                    if tt == 0 and b4 == 0:
                        wts += list(v_p1_all)
                    if tt == 0 and b4 == 1:
                        wts += [("v", S.val("v")), ("s", S.val("s")),
                                ("gm", S.val("gm"))]
                    for c in range(MC):
                        tvv = S.op("tensor", lambda E_, cc=c, t=tt: E_.matmul(
                            psQ1, xnT[:, T * cc + 128 * t:T * cc + 128 * (t + 1)],
                            wvb[:, 512 * cc:512 * (cc + 1)],
                            start=(cc == 0), stop=(cc == MC - 1)),
                            inc="t" if c == MC - 1 else None, waits=wts if c == 0 else ())
                    for hh in range(4):
                        sl = (4 * b4 + hh) % 5
                        last_evac = S.op("vector", lambda E_, s_=sl, hh_=hh, t=tt: E_.tensor_copy(
                            out=VSLOT[s_][:, 128 * t:128 * (t + 1)],
                            in_=psQ1[:, 128 * hh_:128 * (hh_ + 1)]), inc="v",
                            waits=(("t", tvv),))
                        vslot_evac[sl] = last_evac
                psq_guard_v = last_evac

            v_batch(0)

            for h in range(0 if _DBG.get('p2cut') == 'vbatch'
                           else int(_DBG.get('nheads', H))):
                par = h % 2
                lq = dma("sync", "qa", wstage[:, 0:1024],
                         wqk_d[0, h].rearrange("p c d -> p (c d)"),
                         waits=(("v", stage_free_v),))
                lk = dma("sync", "qb", wstage[:, 1024:2048],
                         wqk_d[1, h].rearrange("p c d -> p (c d)"))
                cq = S.op("vector", lambda E_, p_=par: E_.tensor_copy(
                    out=wqh[p_][:], in_=wstage[:, 0:1024]), inc="v",
                    waits=(("qa", lq), ("t", w_guard[par])))
                ck = S.op("vector", lambda E_, p_=par: E_.tensor_copy(
                    out=wkh[p_][:], in_=wstage[:, 1024:2048]), inc="v",
                    waits=(("qb", lk), ("t", w_guard[par])))
                stage_free_v = ck

                if h == 4:
                    v_batch(1)
                skip_flash = bool(_DBG.get('noflash'))

                qT_ready = [0] * NTB
                kT_ready = [0] * NTB
                for qk in range(int(_DBG.get("nqk", 2))):
                    wbuf = wqh[par] if qk == 0 else wkh[par]
                    wready = cq if qk == 0 else ck
                    dst = qTb if qk == 0 else kTb
                    msrow = rowap(0, T, BF16)
                    rope_v = [0] * NTB
                    last_ms_evac = 0
                    cut = _DBG.get("p2cut")
                    for tb in range(int(_DBG.get("ntb", NTB))):
                        pq = psQ1 if tb % 2 == 0 else psQ2
                        wts = [("v", wready)]
                        if tb >= 2:
                            wts.append(("v", rope_v[tb - 2]))
                        else:
                            wts += [("v", S.val("v")), ("s", S.val("s")),
                                    ("gm", S.val("gm"))]
                        for c in range(MC):
                            tvv = S.op("tensor", lambda E_, cc=c, w_=wbuf, p_=pq, t_=tb: E_.matmul(
                                p_[:], w_[:, 128 * cc:128 * (cc + 1)],
                                xnT[:, T * cc + 512 * t_:T * cc + 512 * (t_ + 1)],
                                start=(cc == 0), stop=(cc == MC - 1)),
                                inc="t" if c == MC - 1 else None,
                                waits=wts if c == 0 else ())
                        mmv = tvv
                        if cut == "projonly":
                            rope_v[tb] = S.op("vector", lambda E_: E_.tensor_copy(
                                out=mxtmp[:], in_=mxacc[:]), inc="v",
                                waits=(("t", mmv),))
                            continue
                        sqv = S.op("scalar", lambda E_, p_=pq: E_.activation(
                            out=sqb[:], in_=p_[:], func=AF.Square), inc="s",
                            waits=(("t", mmv),))
                        if cut == "sq2":
                            rope_v[tb] = S.op("vector", lambda E_: E_.tensor_copy(
                                out=mxtmp[:], in_=mxacc[:]), inc="v",
                                waits=(("s", sqv),))
                            continue
                        qrb = qsw[:, 0:512] if tb % 2 == 0 else qsw[:, 512:1024]
                        swv = S.op("scalar", lambda E_, p_=pq, qr=qrb: E_.activation(
                            out=qr, in_=p_[:], func=AF.Copy), inc="s",
                            waits=(("v", prev_vv2),))
                        if cut in ("norope", "sqonly"):
                            rope_v[tb] = S.op("vector", lambda E_: E_.tensor_copy(
                                out=mxtmp[:], in_=mxacc[:]), inc="v",
                                waits=(("s", swv),))
                            continue
                        if cut != "noms":
                            msv = S.op("tensor", lambda E_: E_.matmul(
                                psMS[0:1, :], onescol[:], sqb[:], start=True, stop=True),
                                inc="t", waits=(("s", sqv), ("v", last_ms_evac)))
                            last_ms_evac = S.op("vector", lambda E_, t_=tb, mr=msrow: E_.tensor_copy(
                                out=mr[0:1, 512 * t_:512 * (t_ + 1)], in_=psMS[0:1, :]),
                                inc="v", waits=(("t", msv),) + ((("gr", last_g1),) if tb == 0 else ()))
                        psw = psA if tb % 2 == 0 else psB
                        swt = S.op("tensor", lambda E_, qr=qrb, p_=psw: E_.matmul(
                            p_[:], swapm[:], qr, start=True, stop=True), inc="t",
                            waits=(("s", swv), ("v", psw_guard[tb % 2])))
                        m1v = wreg(65536 + 2048 * (tb % 2), 512, BF16)
                        vv1 = S.op("vector", lambda E_, t_=tb, m_=m1v, qr=qrb: E_.tensor_tensor(
                            out=m_[:], in0=qr, in1=tabC[:, 512 * t_:512 * (t_ + 1)],
                            op=ALU.mult), inc="vv", waits=(("s", swv),))
                        prev_vv2 = psw_guard[tb % 2] = S.op(
                            "vector", lambda E_, t_=tb, p_=psw, d_=dst: E_.tensor_tensor(
                                out=d_[:, 512 * t_:512 * (t_ + 1)], in0=p_[:],
                                in1=tabS[:, 512 * t_:512 * (t_ + 1)], op=ALU.mult),
                            inc="v", waits=(("t", swt), ("v", prev_rope)))
                        rope_v[tb] = prev_rope = S.op(
                            "vector", lambda E_, d_=dst, t_=tb, m_=m1v: E_.tensor_tensor(
                                out=d_[:, 512 * t_:512 * (t_ + 1)],
                                in0=d_[:, 512 * t_:512 * (t_ + 1)], in1=m_[:],
                                op=ALU.add), inc="v", waits=(("vv", vv1), ("v", prev_vv2)))
                    if cut in ("projonly", "noroundtrip"):
                        w_guard[par] = mmv
                        continue
                    g1 = last_g1 = S.op("gpsimd", lambda E_, mr=msrow, k_=qk: E_.dma_start(
                        out=rowscr[k_, :], in_=mr[0:1, :]), inc="gr", by=16,
                        waits=(("v", last_ms_evac), ("t", S.val("t"))))
                    g2 = S.op("gpsimd", lambda E_, k_=qk: E_.dma_start(
                        out=msqb[:], in_=rowscr[k_].rearrange("(p c) -> p c", c=32)),
                        inc="gr", by=16, waits=(("gr", g1), ("s", last_s2v)))
                    sl1 = S.op("scalar", lambda E_: E_.activation(
                        out=msq[:], in_=msqb[:], func=AF.Ln, scale=1.0 / DH,
                        bias=consts[:, 2:3]), inc="ss", waits=(("gr", g2),))
                    last_s2v = S.op("scalar", lambda E_: E_.activation(
                        out=msqb[:], in_=msq[:], func=AF.Exp, scale=-0.5), inc="s",
                        waits=(("ss", sl1),))
                    g3 = S.op("gpsimd", lambda E_, k_=qk: E_.dma_start(
                        out=rowscr2[k_, :], in_=msqb[:]), inc="gr", by=16,
                        waits=(("s", last_s2v),))
                    rsrow = A_W[0:1, 79872:88064].bitcast(BF16)
                    g4 = S.op("gpsimd", lambda E_, k_=qk, rr=rsrow: E_.dma_start(
                        out=rr[0:1, :], in_=rowscr2[k_:k_ + 1, :]), inc="gr", by=16,
                        waits=(("gr", g3), ("t", S.val("t"))))
                    for tb in range(NTB):
                        bcv = S.op("tensor", lambda E_, t_=tb, rr=rsrow: E_.matmul(
                            psBC[:], onesrow[:], rr[0:1, 512 * t_:512 * (t_ + 1)],
                            start=True, stop=True), inc="t",
                            waits=(("gr", g4), ("v", rope_v[tb])))
                        rope_v[tb] = S.op("vector", lambda E_, d_=dst, t_=tb: E_.tensor_tensor(
                            out=d_[:, 512 * t_:512 * (t_ + 1)],
                            in0=d_[:, 512 * t_:512 * (t_ + 1)], in1=psBC[:], op=ALU.mult),
                            inc="v", waits=(("t", bcv),))
                        S.wait("tensor", ("v", rope_v[tb]))
                    w_guard[par] = mmv
                    if qk == 0:
                        qT_ready = rope_v[:]
                    else:
                        kT_ready = rope_v[:]

                if "dbg_qt" in dbg_outs and h == _DBG.get("head", 0):
                    S.op("gpsimd", lambda E_: E_.dma_start(
                        out=dbg_outs["dbg_qt"][0], in_=qTb[:]), inc="gm", by=16,
                        waits=(("v", qT_ready[NTB - 1]),))
                    S.op("gpsimd", lambda E_: E_.dma_start(
                        out=dbg_outs["dbg_qt"][1], in_=kTb[:]), inc="gm", by=16,
                        waits=(("v", kT_ready[NTB - 1]),))
                if "dbg_v" in dbg_outs and h == _DBG.get("head", 0):
                    S.op("gpsimd", lambda E_, s_=h % 5: E_.dma_start(
                        out=dbg_outs["dbg_v"], in_=VSLOT[s_][:]), inc="gm", by=16,
                        waits=(("v", vslot_evac[h % 5]),))

                if _DBG.get('noflash'):
                    prev_head_dummy = 0
                vsl = VSLOT[h % 5]
                for qb in range(0 if not _DBG.get('noflash') else NTB, NTB):
                    nkt = 4 * qb + 4
                    for kp in range(nkt // 2):
                        k0, k1 = 2 * kp, 2 * kp + 1
                        ps_s = psS1 if kp % 2 == 0 else psS2
                        S.op("tensor", lambda E_, k_=k0, q_=qb, p_=ps_s: E_.matmul(
                            p_[:, 0:512], kTb[:, 128 * k_:128 * (k_ + 1)],
                            qTb[:, 512 * q_:512 * (q_ + 1)], start=True, stop=True),
                            waits=(("v", kT_ready[k0 // 4]),
                                   ("v", qT_ready[qb]), ("s", sfree[kp % 2])))
                        smm = S.op("tensor", lambda E_, k_=k1, q_=qb, p_=ps_s: E_.matmul(
                            p_[:, 512:1024], kTb[:, 128 * k_:128 * (k_ + 1)],
                            qTb[:, 512 * q_:512 * (q_ + 1)], start=True, stop=True),
                            inc="t")
                        pbuf = PTD[kp % 2]
                        ev = S.op("scalar", lambda E_, p_=ps_s, pb=pbuf: E_.activation(
                            out=pb[:], in_=p_[:], func=AF.Exp, scale=1.0 / DH), inc="s",
                            waits=(("t", smm), ("t", PT_free[kp % 2])))
                        sfree[kp % 2] = ev
                        pready = ("s", ev)
                        j = k0 - 4 * qb
                        if j >= 0:
                            mv = S.op("vector", lambda E_, pb=pbuf, j_=j: E_.tensor_tensor(
                                out=pb[:], in0=pb[:], in1=masks[:, 512 * j_:512 * (j_ + 2)],
                                op=ALU.mult), inc="v", waits=(("s", ev),))
                            pready = ("v", mv)
                        S.op("tensor", lambda E_, k_=k0, pb=pbuf, v_=vsl,
                             st_=(k0 == 0): E_.matmul(
                            psO[:], v_[:, 128 * k_:128 * (k_ + 1)], pb[:, 0:512],
                            start=st_, stop=False),
                            waits=(pready,) + ((("v", prev_obuf_v),) if k0 == 0 else ()))
                        S.op("tensor", lambda E_, k_=k1, pb=pbuf, v_=vsl,
                             sp_=(k1 == nkt - 1): E_.matmul(
                            psO[:], v_[:, 128 * k_:128 * (k_ + 1)], pb[:, 512:1024],
                            start=False, stop=sp_))
                        S.op("tensor", lambda E_, pb=pbuf, st_=(k0 == 0): E_.matmul(
                            psR[0:1, :], onescol[:], pb[:, 0:512],
                            start=st_, stop=False))
                        PT_free[kp % 2] = S.op("tensor", lambda E_, pb=pbuf,
                             sp_=(k1 == nkt - 1): E_.matmul(
                            psR[0:1, :], onescol[:], pb[:, 512:1024],
                            start=False, stop=sp_), inc="t")
                    def _rcopy(E_):
                        with nc.allow_low_precision(reason="r row feeds bf16 PE bcast"):
                            return E_.tensor_copy(
                                out=rowap(0, 512, BF16, byteoff=8192)[0:1, :],
                                in_=psR[0:1, :])
                    rv = S.op("vector", _rcopy, inc="v",
                              waits=(("t", PT_free[(nkt // 2 - 1) % 2]),))
                    bcv = S.op("tensor", lambda E_: E_.matmul(
                        psBC[:], onesrow[:], rowap(0, 512, BF16, byteoff=8192)[0:1, :],
                        start=True, stop=True), inc="t", waits=(("v", rv),))

                    def _rinv(E_):
                        with nc.allow_low_precision(reason="1/r bcast in bf16"):
                            return E_.reciprocal(out=binv[:], in_=psBC[:])
                    sbv = S.op("vector", _rinv, inc="v", waits=(("t", bcv),))
                    S.wait("tensor", ("v", sbv))
                    prev_obuf_v = S.op("vector", lambda E_: E_.tensor_tensor(
                        out=obuf[:], in0=psO[:], in1=binv[:], op=ALU.mult), inc="v",
                        waits=(("v", sbv), ("go", prev_obuf_g)))
                    prev_obuf_g = S.op("gpsimd", lambda E_, h_=h, q_=qb: E_.dma_start(
                        out=otsp[h_, 4 * q_:4 * q_ + 4].rearrange("t p f -> p t f"),
                        in_=obuf[:].rearrange("p (t f) -> p t f", f=128)), inc="go", by=16,
                        waits=(("v", prev_obuf_v),))
            p2_ot_g = prev_obuf_g
            if "dbg_ot" in dbg_outs:
                S.op("gpsimd", lambda E_: E_.dma_start(
                    out=dbg_outs["dbg_ot"], in_=otsp), inc="gm", by=16,
                    waits=(("go", S.val("go")),))

        if "p3" in _STAGES:
            lcx = dma("sync", "qc", scslot(2), clipx_d, waits=S.allw("q"))
            tb1 = S.op("tensor", lambda E_: E_.matmul(
                psBC[:, 0:2], onesrowf[:], scal[0:1, 1:3], start=True, stop=True),
                inc="t", waits=(("qc", lcx), ("v", v_mxg)) + S.allw("t", "v"))
            c128 = sb_c128
            c1 = S.op("vector", lambda E_: E_.tensor_copy(
                out=c128[:, 0:2], in_=psBC[:, 0:2]), inc="vv", waits=(("t", tb1),))
            sv = S.op("scalar", lambda E_: E_.activation(
                out=c128[:, 2:3], in_=c128[:, 1:2], func=AF.Exp, scale=-1.0), inc="s",
                waits=(("vv", c1),))
            c1 = S.op("vector", lambda E_: E_.tensor_scalar_add(
                out=c128[:, 2:3], in0=c128[:, 2:3], scalar1=1.0), inc="vv",
                waits=(("s", sv),))
            c1 = S.op("vector", lambda E_: E_.reciprocal(
                out=c128[:, 3:4], in_=c128[:, 2:3]), inc="vv", waits=(("vv", c1),))
            c1 = S.op("vector", lambda E_: E_.tensor_tensor(
                out=c128[:, 3:4], in0=c128[:, 3:4], in1=c128[:, 0:1], op=ALU.mult),
                inc="vv", waits=(("vv", c1),))
            c1 = S.op("vector", lambda E_: E_.tensor_scalar_mul(
                out=s1pair[:, 0:1], in0=c128[:, 3:4], scalar1=1.0 / 127.0), inc="vv",
                waits=(("vv", c1),))
            v_inv1 = S.op("vector", lambda E_: E_.reciprocal(
                out=s1pair[:, 1:2], in_=s1pair[:, 0:1]), inc="v", waits=(("vv", c1),))

            xq_bf = [act(16384, D, BF16), act(18432, D, BF16)]
            v_p2t_all = S.val("t")
            last_q2 = [0, 0]
            last_s1 = [0, 0]
            last_v2v = [0, 0]
            p2_guards = S.allw("q")
            for tt in range(NTT):
                b = tt % 2
                qs = "qa" if b == 0 else "qb"
                q2s = "q2a" if b == 0 else "q2b"
                wts = [("s", last_s1[b]), ("v", last_v2v[b])]
                if tt < 2:
                    wts += list(p2_guards)
                lv = dma("sync", qs, xtile[b][:], xnsp[128 * tt:128 * (tt + 1), :],
                         waits=wts)
                s1v = S.op("scalar", lambda E_, xb=xtile[b], ob=xn32[b]: E_.activation(
                    out=ob[:], in_=xb[:], func=AF.Identity, scale=inv1vec,
                    bias=consts[:, 0:1]), inc="s",
                    waits=((qs, lv), ("v", v_inv1)))
                last_s1[b] = s1v
                rr1 = S.op("vector", lambda E_, ob=xn32[b]: E_.tensor_scalar(
                    out=ob[:], in0=ob[:], scalar1=-MAGIC, scalar2=127.0,
                    op0=ALU.add, op1=ALU.min), inc="vv", waits=(("s", s1v),))
                q2s = "q2a" if b == 0 else "q2b"
                v2v = S.op("vector", lambda E_, ob=xn32[b], qb_=xq_bf[b]: E_.tensor_scalar_max(
                    out=qb_[:], in0=ob[:], scalar1=-128.0), inc="v",
                    waits=((q2s, last_q2[b]), ("vv", rr1)))
                last_v2v[b] = v2v
                for c in range(MC):
                    twts = [("v", v2v)] if c == 0 else []
                    if tt == 0 and c == 0:
                        twts.append(("t", v_p2t_all))
                    last_q2[b] = S.op("scalar", lambda E_, qb_=xq_bf[b], cc=c, t=tt: E_.dma_start_transpose(
                        xnT[:, T * cc + 128 * t:T * cc + 128 * (t + 1)],
                        qb_[:, 128 * cc:128 * (cc + 1)]), inc=q2s, by=16,
                        waits=twts)
            v_xq_a, v_xq_b = S.val("q2a"), S.val("q2b")
            if "dbg_xqt" in dbg_outs:
                S.op("gpsimd", lambda E_: E_.dma_start(
                    out=dbg_outs["dbg_xqt"], in_=xnT[:]), inc="gm", by=16,
                    waits=(("q2a", v_xq_a), ("q2b", v_xq_b)))

        v_xq_ab = (("q2a", S.val("q2a")), ("q2b", S.val("q2b")))
        if "dbg_sc" in dbg_outs:
            lgv = S.op("gpsimd", lambda E_: E_.dma_start(
                out=row8[0:1, :], in_=cc1o.ap()), inc="gm", by=16,
                waits=(("v", S.val("v")), ("s", S.val("s"))))
            lvv = S.op("vector", lambda E_: E_.tensor_copy(
                out=scal[0:1, 12:13], in_=row8[0:1, 0:1]), inc="v",
                waits=(("gm", lgv),))
            lgv = S.op("gpsimd", lambda E_: E_.dma_start(
                out=row8[0:1, :], in_=cc1i.ap()), inc="gm", by=16,
                waits=(("v", lvv),))
            lvv = S.op("vector", lambda E_: E_.tensor_copy(
                out=scal[0:1, 13:14], in_=row8[0:1, 0:1]), inc="v",
                waits=(("gm", lgv),))
            S.op("gpsimd", lambda E_: E_.dma_start(
                out=dbg_outs["dbg_sc"], in_=scal[0:1, :]), inc="gm", by=16,
                waits=(("v", lvv),))

        def wquant(w_nat_d, w_t_d, dst_off, sc_off, v_prev, q2_guard):
            stage_t = act(0, I, F32)
            stage_n = act(11264, D, F32)
            inv_bc = act(11264, I, F32)
            rmcol = misc(RM_OFF, 22, F32)
            lastv = v_prev
            for ic in range(NI):
                wts = [("v", lastv)]
                if ic == 0 and q2_guard:
                    wts += list(q2_guard)
                lv = dma("sync", "qa", stage_n[:], w_nat_d[128 * ic:128 * (ic + 1), :],
                         waits=wts)
                lastv = S.op("vector", lambda E_, i_=ic: E_.reduce_max(
                    out=rmcol[:, i_:i_ + 1], in_=stage_n[:], axis=AX.X,
                    apply_absolute_value=True), inc="v", waits=(("qa", lv),))
            c2 = S.op("vector", lambda E_: E_.tensor_scalar_mul(
                out=misc(sc_off, 22, F32)[:], in0=rmcol[:], scalar1=1.0 / 127.0),
                inc="vv", waits=(("v", lastv),))
            c2 = S.op("vector", lambda E_: E_.reciprocal(out=rmcol[:], in_=rmcol[:]),
                      inc="vv", waits=(("vv", c2),))
            vv = S.op("vector", lambda E_: E_.tensor_scalar_mul(
                out=rmcol[:], in0=rmcol[:], scalar1=127.0), inc="v", waits=(("vv", c2),))
            gl = 0
            for ic in range(NI):
                gl = S.op("gpsimd", lambda E_, i_=ic: E_.dma_start(
                    out=wscr[0, 128 * i_:128 * (i_ + 1)], in_=rmcol[:, i_:i_ + 1]),
                    inc="gm", by=16, waits=(("v", vv),) if ic == 0 else ())
            invrow = rowap(0, I, F32)
            gl2 = S.op("gpsimd", lambda E_: E_.dma_start(
                out=invrow[0:1, :], in_=wscr[0:1, 0:I]), inc="gm", by=16,
                waits=(("gm", gl),))
            lastbc = 0
            for j in range((I + 511) // 512):
                n = min(512, I - 512 * j)
                tbv = S.op("tensor", lambda E_, j_=j, n_=n: E_.matmul(
                    psBC[:, 0:n_], onesrowf[:], invrow[0:1, 512 * j_:512 * j_ + n_],
                    start=True, stop=True), inc="t",
                    waits=(("gm", gl2), ("v", lastbc)) + (S.allw("t", "v") if j == 0 else ()))
                lastbc = S.op("vector", lambda E_, j_=j, n_=n: E_.tensor_copy(
                    out=inv_bc[:, 512 * j_:512 * j_ + n_], in_=psBC[:, 0:n_]), inc="v",
                    waits=(("t", tbv),))
            lastq = 0
            for c in range(MC):
                lv = dma("sync", "qb", stage_t[:], w_t_d[128 * c:128 * (c + 1), :],
                         waits=(("v", lastq if lastq else lastbc),))
                q1 = S.op("vector", lambda E_: E_.scalar_tensor_tensor(
                    out=stage_t[:], in0=stage_t[:], scalar=1.0, in1=inv_bc[:],
                    op0=ALU.mult, op1=ALU.mult), inc="vv",
                    waits=(("qb", lv), ("v", lastbc)))
                lastq = S.op("vector", lambda E_, c_=c: E_.tensor_scalar(
                    out=wreg(dst_off + 5632 * c_, I, BF16)[:], in0=stage_t[:],
                    scalar1=MAGIC, scalar2=-MAGIC, op0=ALU.add, op1=ALU.add), inc="v",
                    waits=(("vv", q1),))
            return lastq

        v_wu = 0
        if "p35" in _STAGES:
            v_wg = wquant(wgn_d, wgt_d, 0, GSC_OFF, 0, v_xq_ab)
            v_wu = wquant(wun_d, wut_d, 45056, USC_OFF, v_wg, ())
            S.op("vector", lambda E_: E_.tensor_single_scalar(
                out=misc(GSC_OFF, 22, F32)[:], in_=misc(GSC_OFF, 22, F32)[:],
                scalar=s1pair[:, 0:1], op=ALU.mult), waits=(("v", v_wu), ("v", v_inv1)))
            v_wu = S.op("vector", lambda E_: E_.tensor_single_scalar(
                out=misc(USC_OFF, 22, F32)[:], in_=misc(USC_OFF, 22, F32)[:],
                scalar=s1pair[:, 0:1], op=ALU.mult), inc="v")
            if "dbg_gq" in dbg_outs:
                S.op("gpsimd", lambda E_: E_.dma_start(
                    out=dbg_outs["dbg_gq"], in_=wreg(0, MC * I, BF16)[:]), inc="gm",
                    by=16, waits=(("v", v_wu),))
                S.op("gpsimd", lambda E_: E_.dma_start(
                    out=dbg_outs["dbg_uq"], in_=wreg(45056, MC * I, BF16)[:]), inc="gm",
                    by=16, waits=(("v", v_wu),))

        v_hmg = 0
        if "p4" in _STAGES:
            hrow = [act(0, T, FP16), act(8192, T, FP16)]
            gsb = [act(16384, 512, BF16), act(18432, 512, BF16)]
            usb = [act(17408, 512, BF16), act(19456, 512, BF16)]
            gate_bank = [psA, psO]
            up_bank = [psB, psR]
            bank_free = [0, 0]
            hacc = 0
            hrow_free_g = [0, 0]
            hrow_free_v = [0, 0]
            p4_guards = S.allw()
            for ic in range(NI):
                hb = ic % 2
                for tb in range(NTB):
                    pp = tb % 2
                    pg, pu = gate_bank[pp], up_bank[pp]
                    wts = [("v", v_wu), ("v", bank_free[pp])]
                    if ic == 0 and tb == 0:
                        wts += list(v_xq_ab) + list(p4_guards)
                    for c in range(MC):
                        S.op("tensor", lambda E_, c_=c, i_=ic, t_=tb, p_=pg: E_.matmul(
                            p_[:], wreg(5632 * c_, I, BF16)[:, 128 * i_:128 * (i_ + 1)],
                            xnT[:, T * c_ + 512 * t_:T * c_ + 512 * (t_ + 1)],
                            start=(c_ == 0), stop=(c_ == MC - 1)),
                            waits=wts if c == 0 else ())
                    for c in range(MC):
                        tg = S.op("tensor", lambda E_, c_=c, i_=ic, t_=tb, p_=pu: E_.matmul(
                            p_[:], wreg(45056 + 5632 * c_, I, BF16)[:, 128 * i_:128 * (i_ + 1)],
                            xnT[:, T * c_ + 512 * t_:T * c_ + 512 * (t_ + 1)],
                            start=(c_ == 0), stop=(c_ == MC - 1)),
                            inc="t" if c == MC - 1 else None)
                    mmv = tg
                    S.op("scalar", lambda E_, p_=pg, i_=ic, b_=pp: E_.activation(
                        out=gsb[b_][:], in_=p_[:], func=AF.Silu,
                        scale=misc(GSC_OFF, 22, F32)[:, i_:i_ + 1]), waits=(("t", mmv),))
                    su = S.op("scalar", lambda E_, p_=pu, i_=ic, b_=pp: E_.activation(
                        out=usb[b_][:], in_=p_[:], func=AF.Copy,
                        scale=misc(USC_OFF, 22, F32)[:, i_:i_ + 1]), inc="s")
                    hwts = [("s", su)]
                    if tb == 0 and ic >= 2:
                        hwts += [(("ga" if hb == 0 else "gb"), hrow_free_g[hb]),
                                 ("vv", hrow_free_v[hb])]
                    bank_free[pp] = S.op("vector", lambda E_, b_=pp, hb_=hb, t_=tb: E_.tensor_tensor(
                        out=hrow[hb_][:, 512 * t_:512 * (t_ + 1)], in0=gsb[b_][:],
                        in1=usb[b_][:], op=ALU.mult), inc="v", waits=hwts)
                last_hv = bank_free[(NTB - 1) % 2]
                hrow_free_g[hb] = S.op("gpsimd", lambda E_, i_=ic, hb_=hb: E_.dma_start(
                    out=hsp[i_].rearrange("t p f -> p t f"),
                    in_=hrow[hb_][:].rearrange("p (t f) -> p t f", f=128)),
                    inc=("ga" if hb == 0 else "gb"), by=16, waits=(("v", last_hv),))
                hr = S.op("vector", lambda E_, hb_=hb: E_.reduce_max(
                    out=hmxt[:], in_=hrow[hb_][:], axis=AX.X, apply_absolute_value=True),
                    inc="vv", waits=(("v", last_hv), ("vv", hacc)))
                if ic == 0:
                    hacc = S.op("vector", lambda E_: E_.tensor_copy(
                        out=hmx[:], in_=hmxt[:]), inc="vv", waits=(("vv", hr),))
                else:
                    hacc = S.op("vector", lambda E_: E_.tensor_tensor(
                        out=hmx[:], in0=hmx[:], in1=hmxt[:], op=ALU.max), inc="vv",
                        waits=(("vv", hr),))
                hrow_free_v[hb] = hacc
            if "dbg_h" in dbg_outs:
                S.op("gpsimd", lambda E_: E_.dma_start(
                    out=dbg_outs["dbg_h"], in_=hsp), inc="gm", by=16,
                    waits=(("ga", S.val("ga")), ("gb", S.val("gb"))))
            gv = S.op("gpsimd", lambda E_: E_.dma_start(
                out=mxscr[0, 0:128], in_=hmx[:]), inc="gm", by=16,
                waits=(("v", S.val("v")), ("vv", hacc)))
            gv = S.op("gpsimd", lambda E_: E_.dma_start(
                out=row128[0:1, :], in_=mxscr[0:1, 0:128]), inc="gm", by=16,
                waits=(("gm", gv),))
            vmx = S.op("vector", lambda E_: E_.reduce_max(
                out=scslot(6), in_=row128[0:1, :], axis=AX.X), inc="v",
                waits=(("gm", gv),))
            gv = S.op("gpsimd", lambda E_: E_.dma_start(
                out=cc2i.ap(), in_=ones8f[:]), inc="gm", by=16, waits=(("gm", gv),))
            gv = S.op("gpsimd", lambda E_: E_.dma_start(
                out=cc2i.ap()[0, 0:1], in_=scslot(6)), inc="gm", by=16,
                waits=(("v", vmx), ("gm", gv)))
            arv = S.op("gpsimd", lambda E_: E_.collective_compute(
                "AllReduce", ALU.max, replica_groups=[list(range(8))],
                ins=[cc2i.ap().opt()], outs=[cc2o.ap().opt()]), inc="cc",
                waits=(("gm", gv),))
            gv = S.op("gpsimd", lambda E_: E_.dma_start(
                out=row8[0:1, :], in_=cc2o.ap()), inc="gm", by=16, waits=(("cc", arv),))
            v_hmg = S.op("vector", lambda E_: E_.tensor_copy(
                out=scslot(7), in_=row8[0:1, 0:1]), inc="v", waits=(("gm", gv),))

        v_attn = 0
        v_wd = 0
        v_fbc = 0
        fbc = wreg(61440, D, F32)
        if "p5" in _STAGES:
            wobf = wreg(45056, 8192, BF16)
            wost = act(4096, 4096, F32)
            p5_guards = S.allw()
            cw = 0
            for half in range(2):
                lv = dma("sync", "qa", wost[:],
                         wo_d.rearrange("p h m -> p (h m)")[:, 4096 * half:4096 * (half + 1)],
                         waits=((("v", cw),) if half else p5_guards))
                cw = S.op("vector", lambda E_, h_=half: E_.tensor_copy(
                    out=wobf[:, 4096 * h_:4096 * (h_ + 1)], in_=wost[:]), inc="v",
                    waits=(("qa", lv),))
            otb = [act(0, 1024, BF16), act(2048, 1024, BF16)]
            last_mm = [0, 0]
            prev_evac = 0
            for tt in range(NTT):
                b = tt % 2
                qs5 = "qa" if b == 0 else "qb"
                lv = dma("sync", qs5, otb[b][:].rearrange("p (h f) -> p h f", f=128),
                         otsp[:, tt].rearrange("h p f -> p h f"),
                         waits=(("t", last_mm[b]), ("go", p2_ot_g)) + ((("v", cw),) if tt == 0 else ()))
                for h in range(H):
                    for half in range(2):
                        pw = psQ1 if half == 0 else psQ2
                        wts = []
                        if h == 0 and half == 0:
                            wts = [(qs5, lv), ("v", cw), ("v", prev_evac)]
                        tv = S.op("tensor", lambda E_, h_=h, f_=half, b_=b, p_=pw: E_.matmul(
                            p_[:], otb[b_][:, 128 * h_:128 * (h_ + 1)],
                            wobf[:, D * h_ + 512 * f_:D * h_ + 512 * (f_ + 1)],
                            start=(h_ == 0), stop=(h_ == H - 1)),
                            inc="t" if (h == H - 1 and half == 1) else None,
                            waits=wts if (h == 0 and half == 0) else ())
                last_mm[b] = tv
                for half in range(2):
                    pw = psQ1 if half == 0 else psQ2
                    prev_evac = S.op("vector", lambda E_, t_=tt, f_=half, p_=pw: E_.tensor_copy(
                        out=big(0, NTT * D)[:, D * t_ + 512 * f_:D * t_ + 512 * (f_ + 1)],
                        in_=p_[:]), inc="v", waits=(("t", last_mm[b]),))
            v_attn = prev_evac
            if "dbg_attn" in dbg_outs:
                S.op("gpsimd", lambda E_: E_.dma_start(
                    out=dbg_outs["dbg_attn"], in_=big(0, NTT * D)[:]), inc="gm", by=16,
                    waits=(("v", v_attn),))

            rmcol8 = misc(RM_OFF, 8, F32)
            wdst = act(4096, I, F32)
            lastv = v_attn
            for c in range(MC):
                lv = dma("sync", "qa", wdst[:], wdn_d[128 * c:128 * (c + 1), :],
                         waits=(("v", lastv),))
                lastv = S.op("vector", lambda E_, c_=c: E_.reduce_max(
                    out=rmcol8[:, c_:c_ + 1], in_=wdst[:], axis=AX.X,
                    apply_absolute_value=True), inc="v", waits=(("qa", lv),))
            c3 = S.op("vector", lambda E_: E_.tensor_scalar_mul(
                out=misc(WDSC_OFF, 8, F32)[:], in0=rmcol8[:], scalar1=1.0 / 127.0),
                inc="vv", waits=(("v", lastv),))
            c3 = S.op("vector", lambda E_: E_.reciprocal(out=rmcol8[:], in_=rmcol8[:]),
                      inc="vv", waits=(("vv", c3),))
            vv = S.op("vector", lambda E_: E_.tensor_scalar_mul(
                out=rmcol8[:], in0=rmcol8[:], scalar1=127.0), inc="v", waits=(("vv", c3),))
            gl = 0
            for c in range(MC):
                gl = S.op("gpsimd", lambda E_, c_=c: E_.dma_start(
                    out=wscr[0, 128 * c_:128 * (c_ + 1)], in_=rmcol8[:, c_:c_ + 1]),
                    inc="gm", by=16, waits=(("v", vv),) if c == 0 else ())
            invrow2 = rowap(0, D, F32)
            gl2 = S.op("gpsimd", lambda E_: E_.dma_start(
                out=invrow2[0:1, :], in_=wscr[0:1, 0:D]), inc="gm", by=16,
                waits=(("gm", gl), ("t", S.val("t"))))
            inv_bc2 = act(4096 + 11264, D, F32)
            lastbc = 0
            for j in range(2):
                tbv = S.op("tensor", lambda E_, j_=j: E_.matmul(
                    psBC[:], onesrowf[:], invrow2[0:1, 512 * j_:512 * (j_ + 1)],
                    start=True, stop=True), inc="t",
                    waits=(("gm", gl2), ("v", lastbc if j else v_attn), ("s", S.val("s"))))
                lastbc = S.op("vector", lambda E_, j_=j: E_.tensor_copy(
                    out=inv_bc2[:, 512 * j_:512 * (j_ + 1)], in_=psBC[:]), inc="v",
                    waits=(("t", tbv),))
            lastq = 0
            for ic in range(NI):
                lv = dma("sync", "qb", wdst[:, 0:D], wdt_d[128 * ic:128 * (ic + 1), :],
                         waits=(("v", lastq if lastq else lastbc),))
                q5 = S.op("vector", lambda E_: E_.scalar_tensor_tensor(
                    out=wdst[:, 0:D], in0=wdst[:, 0:D], scalar=1.0, in1=inv_bc2[:],
                    op0=ALU.mult, op1=ALU.mult), inc="vv",
                    waits=(("qb", lv), ("v", lastbc)))
                lastq = S.op("vector", lambda E_, i_=ic: E_.tensor_scalar(
                    out=wreg(2048 * i_, D, BF16)[:], in0=wdst[:, 0:D], scalar1=MAGIC,
                    scalar2=-MAGIC, op0=ALU.add, op1=ALU.add), inc="v",
                    waits=(("vv", q5),))
            v_wd = lastq
            if "dbg_dq" in dbg_outs:
                S.op("gpsimd", lambda E_: E_.dma_start(
                    out=dbg_outs["dbg_dq"], in_=wreg(0, NI * D, BF16)[:]), inc="gm",
                    by=16, waits=(("v", v_wd),))

            lcd = dma("sync", "qc", scslot(8), clipd_d)
            tb1 = S.op("tensor", lambda E_: E_.matmul(
                psBC[:, 0:2], onesrowf[:], scal[0:1, 7:9], start=True, stop=True),
                inc="t", waits=(("qc", lcd), ("v", v_hmg), ("v", lastbc)))
            c128b = sb_c128
            c4 = S.op("vector", lambda E_: E_.tensor_copy(
                out=c128b[:, 4:6], in_=psBC[:, 0:2]), inc="vv", waits=(("t", tb1),))
            sv = S.op("scalar", lambda E_: E_.activation(
                out=c128b[:, 6:7], in_=c128b[:, 5:6], func=AF.Exp, scale=-1.0), inc="s",
                waits=(("vv", c4),))
            c4 = S.op("vector", lambda E_: E_.tensor_scalar_add(
                out=c128b[:, 6:7], in0=c128b[:, 6:7], scalar1=1.0), inc="vv",
                waits=(("s", sv),))
            c4 = S.op("vector", lambda E_: E_.reciprocal(
                out=c128b[:, 7:8], in_=c128b[:, 6:7]), inc="vv", waits=(("vv", c4),))
            c4 = S.op("vector", lambda E_: E_.tensor_tensor(
                out=c128b[:, 7:8], in0=c128b[:, 7:8], in1=c128b[:, 4:5], op=ALU.mult),
                inc="vv", waits=(("vv", c4),))
            c4 = S.op("vector", lambda E_: E_.tensor_scalar_mul(
                out=s2pair[:, 0:1], in0=c128b[:, 7:8], scalar1=1.0 / 127.0), inc="vv",
                waits=(("vv", c4),))
            v_inv2 = S.op("vector", lambda E_: E_.reciprocal(
                out=s2pair[:, 1:2], in_=s2pair[:, 0:1]), inc="v", waits=(("vv", c4),))
            vv = S.op("vector", lambda E_: E_.tensor_single_scalar(
                out=misc(WDSC_OFF, 8, F32)[:], in_=misc(WDSC_OFF, 8, F32)[:],
                scalar=s2pair[:, 0:1], op=ALU.mult), inc="v", waits=(("v", v_inv2),))
            gl = 0
            for c in range(MC):
                gl = S.op("gpsimd", lambda E_, c_=c: E_.dma_start(
                    out=wscr[0, 2048 + 128 * c_:2048 + 128 * (c_ + 1)],
                    in_=misc(WDSC_OFF, 8, F32)[:, c_:c_ + 1]), inc="gm", by=16,
                    waits=(("v", vv),) if c == 0 else ())
            fbrow = rowap(0, D, F32, byteoff=4096)
            gl2 = S.op("gpsimd", lambda E_: E_.dma_start(
                out=fbrow[0:1, :], in_=wscr[0:1, 2048:2048 + D]), inc="gm", by=16,
                waits=(("gm", gl),))
            lastf = 0
            for j in range(2):
                tbv = S.op("tensor", lambda E_, j_=j: E_.matmul(
                    psBC[:], onesrowf[:], fbrow[0:1, 512 * j_:512 * (j_ + 1)],
                    start=True, stop=True), inc="t",
                    waits=(("gm", gl2), ("v", lastf if j else v_inv2), ("s", S.val("s"))))
                lastf = S.op("vector", lambda E_, j_=j: E_.tensor_copy(
                    out=fbc[:, 512 * j_:512 * (j_ + 1)], in_=psBC[:]), inc="v",
                    waits=(("t", tbv),))
            v_fbc = lastf

        if "p6" in _STAGES:
            hq = [act(0, I, BF16), act(5632, I, BF16)]
            h16in = [act(11264, I, FP16), act(16896, I, FP16)]
            qscr = wreg(45056, I, F32)
            xrld = [wreg(65536, D, F32), wreg(69632, D, F32)]
            osum = [wreg(73728, D, F32), wreg(77824, D, F32)]
            last_sq = [0, 0]
            last_hq_t = [0, 0]
            last_e3 = [0, 0]
            last_st = [0, 0]
            prev_vq2 = 0
            prev_evac6 = 0
            p6_guards = S.allw()
            for tt in range(NTT):
                b = tt % 2
                qs6 = "qa" if b == 0 else "qb"
                qx6 = "qc" if b == 0 else "qd"
                os6 = "outa" if b == 0 else "outb"
                wts = [("s", last_sq[b])]
                if tt == 0:
                    wts += list(p6_guards)
                lh = dma("sync", qs6, h16in[b][:].rearrange("p (i f) -> p i f", f=128),
                         hsp[:, tt].rearrange("i p f -> p i f"), waits=wts)
                lx = dma("sync", qx6, xrld[b][:], x_d[128 * tt:128 * (tt + 1), :],
                         waits=(("v", last_e3[b]),))
                sq1 = S.op("scalar", lambda E_, b_=b: E_.activation(
                    out=qscr[:], in_=h16in[b_][:], func=AF.Identity, scale=inv2vec,
                    bias=consts[:, 0:1]), inc="s",
                    waits=((qs6, lh), ("v", v_inv2), ("v", prev_vq2)))
                last_sq[b] = sq1
                vq1 = S.op("vector", lambda E_: E_.tensor_scalar(
                    out=qscr[:], in0=qscr[:], scalar1=-MAGIC, scalar2=127.0,
                    op0=ALU.add, op1=ALU.min), inc="vv", waits=(("s", sq1),))
                vq2 = S.op("vector", lambda E_, b_=b: E_.tensor_scalar_max(
                    out=hq[b_][:], in0=qscr[:], scalar1=-128.0), inc="v",
                    waits=(("t", last_hq_t[b]), ("vv", vq1)))
                prev_vq2 = vq2
                for ic in range(NI):
                    for half in range(2):
                        pd = psQ1 if half == 0 else psQ2
                        wts2 = []
                        if ic == 0 and half == 0:
                            wts2 = [("v", vq2), ("v", v_wd), ("v", v_fbc),
                                    ("v", prev_evac6)]
                        tv = S.op("tensor", lambda E_, i_=ic, f_=half, b_=b, p_=pd: E_.matmul(
                            p_[:], hq[b_][:, 128 * i_:128 * (i_ + 1)],
                            wreg(2048 * i_, D, BF16)[:, 512 * f_:512 * (f_ + 1)],
                            start=(i_ == 0), stop=(i_ == NI - 1)),
                            inc="t" if (ic == NI - 1 and half == 1) else None,
                            waits=wts2)
                last_hq_t[b] = tv
                for half in range(2):
                    pd = psQ1 if half == 0 else psQ2
                    e1 = S.op("vector", lambda E_, f_=half, b_=b, p_=pd: E_.scalar_tensor_tensor(
                        out=osum[b_][:, 512 * f_:512 * (f_ + 1)], in0=p_[:], scalar=1.0,
                        in1=fbc[:, 512 * f_:512 * (f_ + 1)], op0=ALU.mult, op1=ALU.mult),
                        inc="vv", waits=(("t", tv), (qx6, lx), (os6, last_st[b])))
                    e2 = S.op("vector", lambda E_, f_=half, b_=b, t_=tt: E_.tensor_tensor(
                        out=osum[b_][:, 512 * f_:512 * (f_ + 1)],
                        in0=osum[b_][:, 512 * f_:512 * (f_ + 1)],
                        in1=big(0, NTT * D)[:, D * t_ + 512 * f_:D * t_ + 512 * (f_ + 1)],
                        op=ALU.add), inc="vv", waits=(("vv", e1),))
                    prev_evac6 = S.op("vector", lambda E_, f_=half, b_=b: E_.tensor_tensor(
                        out=osum[b_][:, 512 * f_:512 * (f_ + 1)],
                        in0=osum[b_][:, 512 * f_:512 * (f_ + 1)],
                        in1=xrld[b_][:, 512 * f_:512 * (f_ + 1)], op=ALU.add), inc="v",
                        waits=(("vv", e2),))
                last_e3[b] = prev_evac6
                last_st[b] = S.op("gpsimd", lambda E_, b_=b, t_=tt: E_.dma_start(
                    out=out_d[128 * t_:128 * (t_ + 1), :], in_=osum[b_][:]),
                    inc=os6, by=16, waits=(("v", prev_evac6),))
            S.wait("gpsimd", ("outa", S.val("outa")), ("outb", S.val("outb")))
            S.wait("sync", ("outa", S.val("outa")), ("outb", S.val("outb")))

        S.emit(block, sems)

    return nc



def _prep_core_inputs(b, ins):
    x = np.ascontiguousarray(np.asarray(ins["L0_attn_x"])[b], dtype=np.float32)
    wq = np.asarray(ins["L0_attn_wq"], dtype=np.float32)
    wk = np.asarray(ins["L0_attn_wk"], dtype=np.float32)
    wv = np.asarray(ins["L0_attn_wv"], dtype=np.float32).reshape(D, D)
    wo = np.ascontiguousarray(np.asarray(ins["L0_attn_wo"], dtype=np.float32))
    wg = np.asarray(ins["L0_ffn_w_gate"], dtype=np.float32)
    wu = np.asarray(ins["L0_ffn_w_up"], dtype=np.float32)
    wd = np.asarray(ins["L0_ffn_w_down"], dtype=np.float32)

    def qk_r(w):
        return w.reshape(MC, 128, H, DH).transpose(2, 1, 0, 3)

    wqk = np.ascontiguousarray(np.stack([qk_r(wq), qk_r(wk)]))
    wv_r = np.ascontiguousarray(wv.reshape(MC, 128, 2, 512).transpose(2, 1, 0, 3))

    pos = np.arange(T, dtype=np.float64)
    half = DH // 2
    freqs = np.power(10000.0, -np.arange(half, dtype=np.float64) / half)
    rad = pos[None, :] * freqs[:, None]
    cos = np.cos(rad)
    sin = np.sin(rad)
    tabC = np.concatenate([cos, cos], axis=0).astype(ml_dtypes.bfloat16)
    tabS = np.concatenate([-sin, sin], axis=0).astype(ml_dtypes.bfloat16)

    kk = np.arange(128)[:, None]
    qq = np.arange(512)[None, :]
    m = [(qq >= kk + 128 * j).astype(np.float32) for j in range(4)]
    masksv = np.concatenate(m, axis=1).astype(ml_dtypes.bfloat16)

    consts = np.zeros((128, 4), dtype=np.float32)
    consts[:, 0] = MAGIC
    consts[:, 1] = EPS_LN
    consts[:, 2] = EPS_RMS

    return dict(
        x=x,
        wqk=wqk,
        wv=wv_r,
        wo=np.ascontiguousarray(wo.transpose(1, 0, 2)),
        wgn=np.ascontiguousarray(wg),
        wgt=np.ascontiguousarray(wg.T),
        wun=np.ascontiguousarray(wu),
        wut=np.ascontiguousarray(wu.T),
        wdn=np.ascontiguousarray(wd),
        wdt=np.ascontiguousarray(wd.T),
        clipx=np.asarray(ins["L0_ffn_clip_x"], dtype=np.float32).reshape(1, 1),
        clipd=np.asarray(ins["L0_ffn_clip_down"], dtype=np.float32).reshape(1, 1),
        tabC=np.ascontiguousarray(tabC),
        tabS=np.ascontiguousarray(tabS),
        masks=np.ascontiguousarray(masksv),
        ident=np.eye(128, dtype=ml_dtypes.bfloat16),
        swapm=np.ascontiguousarray(np.roll(np.eye(128), 64, axis=0)).astype(ml_dtypes.bfloat16),
        onescol=np.ones((128, 1), dtype=ml_dtypes.bfloat16),
        onesrow=np.ones((1, 128), dtype=ml_dtypes.bfloat16),
        onesrowf=np.ones((1, 128), dtype=np.float32),
        ones8f=np.ones((1, 8), dtype=np.float32),
        consts=consts,
    )


def kernel(**inputs):
    nc = build_program(dbg=_DBG.get("dump", ()))
    in_maps = [_prep_core_inputs(b, inputs) for b in range(8)]
    res = run_bass_kernel_spmd(nc, in_maps, core_ids=list(range(8)))
    if _DBG.get("dump"):
        _DBG["results"] = res.results
    if "p6" not in _STAGES:
        return np.zeros((8, T, D), dtype=np.float32)
    out = np.stack([np.asarray(res.results[b]["out"]) for b in range(8)], axis=0)
    return out.astype(np.float32)
